# version 04 (final)
# baseline (speedup 1.0000x reference)
import numpy as np
import concourse.bass as bass
import concourse.mybir as mb
from concourse.bass_utils import run_bass_kernel_spmd
from contextlib import ExitStack

F32 = mb.dt.float32
F32R = mb.dt.float32r
BF16 = mb.dt.bfloat16
I32 = mb.dt.int32
AF = mb.ActivationFunctionType
OP = mb.AluOpType
AX = mb.AxisListType

ENGINES = ("sync", "scalar", "vector", "tensor", "gpsimd")

B, T, D = 2, 4096, 2048
DEBUG_L3 = False
DEBUG_L8 = False
NCORES = 8
TC = 1024
EPS = 1e-5
NEG = -1e30


class Step:
    __slots__ = ("eng", "emit", "inc", "deps", "need", "post", "semkey")

    def __init__(self, eng, emit, inc, deps):
        self.eng, self.emit, self.inc, self.deps = eng, emit, inc, list(deps)
        self.need = False
        self.post = 0
        self.semkey = eng


class Sched:

    def __init__(self, nc):
        self.nc = nc
        self.steps = {e: [] for e in ENGINES}
        self.fence = []
        self.dma_ctr = {}

    def add(self, eng, emit, deps=(), dma=False):
        s = Step(eng, emit, 16 if dma else 1, list(deps) + self.fence)
        for d in s.deps:
            if d is not None and d.eng != eng:
                d.need = True
        if dma:
            n = self.dma_ctr.get(eng, 0)
            self.dma_ctr[eng] = n + 1
            s.semkey = (eng, n % 8)
        self.steps[eng].append(s)
        return s

    def dma(self, eng, out, in_, deps=()):
        return self.add(eng, lambda e, o=out, i=in_: e.dma_start(out=o, in_=i),
                        deps=deps, dma=True)

    def barrier(self):
        self.fence = [ss[-1] for ss in self.steps.values() if ss]
        for d in self.fence:
            d.need = True
        return list(self.fence)

    def emit(self, ctx):
        keys = set()
        for e in ENGINES:
            for st in self.steps[e]:
                keys.add(st.semkey)
            keys.add(e)
        sems = {k: ctx.enter_context(self.nc.semaphore(f"sem_{k}")) for k in keys}
        for e in ENGINES:
            for st in self.steps[e]:
                if st.inc == 16:
                    st.need = True
            if self.steps[e]:
                self.steps[e][-1].need = True
        counts = {k: 0 for k in keys}
        for e in ENGINES:
            for s in self.steps[e]:
                if s.need:
                    counts[s.semkey] += s.inc
                s.post = counts[s.semkey]
        totals = dict(counts)
        sched = self

        with self.nc.Block() as block:
            def section(eng_name):
                def body(eng):
                    for s in sched.steps[eng_name]:
                        waits = {}
                        for d in s.deps:
                            if d is None:
                                continue
                            if d.eng == eng_name and d.semkey == eng_name:
                                continue
                            if d.eng == eng_name and not isinstance(d.semkey, tuple):
                                continue
                            waits[d.semkey] = max(waits.get(d.semkey, 0), d.post)
                        for kk, v in waits.items():
                            if v > 0:
                                eng.wait_ge(sems[kk], v)
                        inst = s.emit(eng)
                        if s.need:
                            inst.then_inc(sems[s.semkey], s.inc)
                    if eng_name == "gpsimd":
                        for kk, v in totals.items():
                            if kk != "gpsimd" and v > 0:
                                eng.wait_ge(sems[kk], v)
                return body
            block.sync(section("sync"))
            block.scalar(section("scalar"))
            block.vector(section("vector"))
            block.tensor(section("tensor"))
            block.gpsimd(section("gpsimd"))



def stage_lhsT(W):
    K, M = W.shape
    Kp = -(-K // 128) * 128
    Mp = -(-M // 128) * 128
    if (Kp, Mp) != (K, M):
        Wp = np.zeros((Kp, Mp), W.dtype)
        Wp[:K, :M] = W
        W = Wp
    KT, OB = Kp // 128, Mp // 128
    return np.ascontiguousarray(
        W.reshape(KT, 128, OB, 128).transpose(1, 2, 0, 3).reshape(128, OB * KT * 128))


def col(v):
    return np.ascontiguousarray(np.asarray(v, np.float32).reshape(-1, 1))


def row(v):
    return np.ascontiguousarray(np.asarray(v, np.float32).reshape(1, -1))



def load_fm(s, dst, src, n_kt, tcols, eng="sync", deps=()):
    out = []
    for k in range(n_kt):
        out.append(s.dma(eng, dst[:, k * tcols:(k + 1) * tcols],
                         src[k * 128:(k + 1) * 128, :], deps=deps))
    return out


def store_fm(s, dst, src, n_kt, tcols, eng="gpsimd", deps=()):
    out = []
    for k in range(n_kt):
        out.append(s.dma(eng, dst[k * 128:(k + 1) * 128, :],
                         src[:, k * tcols:(k + 1) * tcols], deps=deps))
    return out


def proj(s, nc, w_dram, w_bufs, x_sb, x_kt, x_tc, psums, evac, ob_range,
         t0=0, t1=None, nmax=512, wdeps=(), xdeps=(), kt_sub=None):
    if t1 is None:
        t1 = x_tc
    KT = x_kt if kt_sub is None else kt_sub
    wsz = KT * 128
    last_mm = {}
    last_evac = {}
    prev_load = {}
    steps = []
    for idx, ob in enumerate(ob_range):
        buf = idx % 2
        dep = [last_mm.get(buf)] + list(wdeps)
        ld = s.dma("sync", w_bufs[buf][:, :], w_dram[:, ob * wsz:(ob + 1) * wsz],
                   deps=[d for d in dep if d is not None])
        nch = -(-(t1 - t0) // nmax)
        for ci in range(nch):
            ta = t0 + ci * nmax
            tb = min(t1, ta + nmax)
            pi = (idx * nch + ci) % len(psums)
            ps = psums[pi]
            mm = None
            for kt in range(KT):
                mdeps = []
                if kt == 0:
                    mdeps = [ld, last_evac.get(pi)] + list(xdeps)
                mm = s.add("tensor", lambda e, w=w_bufs[buf], k=kt, a=ta, b=tb, p=ps: e.matmul(
                    p[0:128, 0:b - a], w[:, k * 128:(k + 1) * 128],
                    x_sb[:, k * x_tc + a:k * x_tc + b],
                    start=(k == 0), stop=(k == KT - 1)),
                    deps=[d for d in mdeps if d is not None])
            ev = evac(ob, ps, mm, (ta, tb))
            last_evac[pi] = ev
            last_mm[buf] = mm
        steps.append(ld)
    return steps


def ones_reduce(s, nc, tiles_f32r, ones_col, ps_row, n_tiles, tcols, deps=()):
    mm = None
    for i in range(n_tiles):
        mm = s.add("tensor", lambda e, i=i: e.matmul(
            ps_row[0:1, 0:tcols], ones_col[:, 0:1], tiles_f32r(i),
            start=(i == 0), stop=(i == n_tiles - 1)),
            deps=(list(deps) if i == 0 else []))
    return mm


def bcast_row(s, nc, row_f32r, ones_row, ps, out_sb, tcols, deps=(), nmax=512):
    ev = None
    for ci in range(-(-tcols // nmax)):
        a, b = ci * nmax, min(tcols, (ci + 1) * nmax)
        mm = s.add("tensor", lambda e, a=a, b=b: e.matmul(
            ps[0:128, 0:b - a], ones_row[0:1, :], row_f32r[0:1, a:b],
            start=True, stop=True), deps=deps)
        ev = s.add("vector", lambda e, a=a, b=b: e.tensor_copy(
            out_sb[:, a:b], ps[0:128, 0:b - a]), deps=[mm])
    return ev


RSTD_SCALE = {}


def rstd_row(s, nc, ps_row, sum_sb, rstd_sb, inv_n, tcols, stop, eps=EPS, extra_scale=1.0):
    t1 = s.add("scalar", lambda e: e.activation(
        sum_sb[0:1, 0:tcols], ps_row[0:1, 0:tcols], AF.Ln,
        scale=float(inv_n), bias=float(eps)), deps=[stop])
    fn = AF.Exp
    t2 = s.add("scalar", lambda e: e.activation(
        rstd_sb[0:1, 0:tcols], sum_sb[0:1, 0:tcols], fn, scale=-0.5,
        bias=float(np.log(extra_scale) * 1.0) if extra_scale != 1.0 else ...),
        deps=[t1])
    return t2


def _gather_fm(results, key, rows):
    out = np.zeros((2, rows, T), np.float32)
    for c in range(NCORES):
        b, q = c // 4, c % 4
        out[b][:, q * TC:(q + 1) * TC] = np.asarray(results[c][key], np.float32)
    return out


def kernel(**inputs):
    import ml_dtypes
    inp = {k: np.asarray(v) for k, v in inputs.items()}
    x = inp["L0_attn_hidden_states"]
    ones_c = np.ones((128, 1), np.float32)
    ones_r = np.ones((1, 128), np.float32)
    ones_cb = np.ones((128, 1), ml_dtypes.bfloat16)

    nc = build_l0d()
    wq_all = np.concatenate([inp["L0_ffn_wq_gate"].ravel(), inp["L0_ffn_wq_up"].ravel(),
                             inp["L0_ffn_wq_down"].ravel()]).astype(np.int32)
    maps = [dict(wq=np.ascontiguousarray(wq_all[c * DQC:(c + 1) * DQC].reshape(128, DQW)))
            for c in range(NCORES)]
    r = run_bass_kernel_spmd(nc, maps, core_ids=list(range(NCORES)))
    dq = np.concatenate([np.asarray(r.results[c]["dq"]).ravel() for c in range(NCORES)])
    n1 = 5632 * 2048
    dqg = dq[:n1].reshape(5632, 2048)
    dqu = dq[n1:2 * n1].reshape(5632, 2048)
    dqd = dq[2 * n1:].reshape(2048, 5632)

    nc = build_l1()
    Wst = {p: stage_lhsT(inp[w]) for p, w in
           (("f", "L0_attn_Wf"), ("i", "L0_attn_Wi"), ("g", "L0_attn_Wg"))}
    maps = []
    for c in range(NCORES):
        b, q = c // 4, c % 4
        maps.append(dict(xt=np.ascontiguousarray(x[b, q * TC:(q + 1) * TC, :].T),
                         wf=Wst["f"], wi=Wst["i"], wg=Wst["g"]))
    r = run_bass_kernel_spmd(nc, maps, core_ids=list(range(NCORES)))
    A = _gather_fm(r.results, "a", D)
    I_ = _gather_fm(r.results, "i", D)
    GS = _gather_fm(r.results, "gs", D)

    nc = build_l2()
    maps = []
    for c in range(NCORES):
        b, cb = c // 4, c % 4
        sl = slice(cb * 512, (cb + 1) * 512)
        maps.append(dict(a=np.ascontiguousarray(A[b][sl]),
                         i=np.ascontiguousarray(I_[b][sl])))
    r = run_bass_kernel_spmd(nc, maps, core_ids=list(range(NCORES)))
    O = np.zeros((2, D, T), np.float32)
    for c in range(NCORES):
        b, cb = c // 4, c % 4
        O[b][cb * 512:(cb + 1) * 512] = np.asarray(r.results[c]["o"])

    nc = build_l3()
    DGst = stage_lhsT(np.ascontiguousarray(dqg.T))
    DUst = stage_lhsT(np.ascontiguousarray(dqu.T))
    DDst = stage_lhsT(np.ascontiguousarray(dqd.T))
    WOst = stage_lhsT(inp["L0_attn_Wo"])
    gnw_g = np.ascontiguousarray(inp["L0_attn_g_norm_weight"].reshape(16, 128).T)
    sg_g = np.ascontiguousarray(inp["L0_ffn_s_gate"].reshape(44, 128).T)
    su_g = np.ascontiguousarray(inp["L0_ffn_s_up"].reshape(44, 128).T)
    sd_g = np.ascontiguousarray(inp["L0_ffn_s_down"].reshape(16, 128).T)
    maps = []
    for c in range(NCORES):
        b, q = c // 4, c % 4
        sl = slice(q * TC, (q + 1) * TC)
        maps.append(dict(o=np.ascontiguousarray(O[b][:, sl]),
                         gs=np.ascontiguousarray(GS[b][:, sl]),
                         xt=np.ascontiguousarray(x[b, sl, :].T),
                         gnw=gnw_g, wo=WOst, dg=DGst, du=DUst, dd=DDst,
                         sg=sg_g, su=su_g, sd=sd_g, onesc=ones_c, onesr=ones_r))
    r = run_bass_kernel_spmd(nc, maps, core_ids=list(range(NCORES)))
    X2 = _gather_fm(r.results, "x2", D)

    nc = build_l4()
    WIPst = stage_lhsT(np.ascontiguousarray(inp["L1_attn_in_proj_w"].T))
    cw = inp["L1_attn_conv_w"]
    CONVWg = np.ascontiguousarray(cw.reshape(32, 128, 4).transpose(1, 0, 2).reshape(128, 128))
    CONVBg = np.ascontiguousarray(inp["L1_attn_conv_b"].reshape(32, 128).T)
    XPWst = stage_lhsT(np.ascontiguousarray(inp["L1_attn_x_proj_w"].T))
    DTWst = stage_lhsT(np.ascontiguousarray(inp["L1_attn_dt_proj_w"].T))
    DTBg = np.ascontiguousarray(inp["L1_attn_dt_proj_b"].reshape(32, 128).T)
    MNWg = np.ascontiguousarray(inp["L1_attn_mamba_norm_w"].reshape(16, 128).T)
    maps = []
    for c in range(NCORES):
        b, q = c // 4, c % 4
        t0 = q * TC
        if q == 0:
            halo = np.zeros((D, 4), np.float32)
        else:
            halo = X2[b][:, t0 - 4:t0]
        x2h = np.ascontiguousarray(np.concatenate([halo, X2[b][:, t0:t0 + TC]], axis=1))
        maps.append(dict(x2h=x2h, mnw=MNWg, wip=WIPst, convw=CONVWg, convb=CONVBg,
                         xpw=XPWst, dtw=DTWst, dtb=DTBg, onesc=ones_c, onesr=ones_r))
    r = run_bass_kernel_spmd(nc, maps, core_ids=list(range(NCORES)))
    XS = _gather_fm(r.results, "xs", 4096)
    DT = _gather_fm(r.results, "dt", 4096)
    ZS = _gather_fm(r.results, "zs", 4096)
    BC = _gather_fm(r.results, "bc", 32)

    nc = build_l5()
    maps = []
    for c in range(NCORES):
        b, db = c // 4, c % 4
        rows = slice(db * 1024, (db + 1) * 1024)
        al = inp["L1_attn_A_log"][rows].reshape(8, 128, 16).transpose(1, 0, 2).reshape(128, 128)
        dpg = inp["L1_attn_D_param"][rows].reshape(8, 128).T
        maps.append(dict(xs=np.ascontiguousarray(XS[b][rows]),
                         dt=np.ascontiguousarray(DT[b][rows]),
                         bc=np.ascontiguousarray(BC[b]),
                         alog=np.ascontiguousarray(al), dp=np.ascontiguousarray(dpg)))
    r = run_bass_kernel_spmd(nc, maps, core_ids=list(range(NCORES)))
    Y = np.zeros((2, 4096, T), np.float32)
    for c in range(NCORES):
        b, db = c // 4, c % 4
        Y[b][db * 1024:(db + 1) * 1024] = np.asarray(r.results[c]["y"])

    nc = build_l6()
    WMOst = stage_lhsT(np.ascontiguousarray(inp["L1_attn_mamba_out_w"].T))
    WQKVst = stage_lhsT(np.ascontiguousarray(inp["L1_attn_wqkv_w"].T))
    ANWg = np.ascontiguousarray(inp["L1_attn_attn_norm_w"].reshape(16, 128).T)
    QKVBg = np.ascontiguousarray(inp["L1_attn_wqkv_b"].reshape(24, 128).T)
    maps = []
    for c in range(NCORES):
        b, q = c // 4, c % 4
        sl = slice(q * TC, (q + 1) * TC)
        maps.append(dict(y=np.ascontiguousarray(Y[b][:, sl]),
                         zs=np.ascontiguousarray(ZS[b][:, sl]),
                         x2=np.ascontiguousarray(X2[b][:, sl]),
                         anw=ANWg, wmo=WMOst, wqkv=WQKVst, qkvb=QKVBg,
                         onescb=ones_cb, onesr=ones_r))
    r = run_bass_kernel_spmd(nc, maps, core_ids=list(range(NCORES)))
    H = _gather_fm(r.results, "h", D)
    QKV = _gather_fm(r.results, "qkv", 3072)

    nc = build_l7()
    masks = np.zeros((128, 8 * 512), np.float32)
    for mi, dq_ in enumerate([0, 1, 2, 3, 16, 17, 18, 19]):
        rr = np.arange(128)[:, None]
        cc = np.arange(512)[None, :]
        dlt = dq_ * 128 + rr - cc
        masks[:, mi * 512:(mi + 1) * 512] = np.where((dlt < 0) | (dlt >= 2048), NEG, 0.0)
    idb = np.eye(128, dtype=ml_dtypes.bfloat16)
    subw = np.ascontiguousarray(inp["L1_attn_subln_w"].reshape(2, 128).T)
    lamq = np.ascontiguousarray(np.concatenate(
        [inp["L1_attn_lambda_q1"], inp["L1_attn_lambda_q2"]]).reshape(1, 256))
    lamk = np.ascontiguousarray(np.concatenate(
        [inp["L1_attn_lambda_k1"], inp["L1_attn_lambda_k2"]]).reshape(1, 256))
    maps = []
    for c in range(NCORES):
        b, pg = c // 4, c % 4
        j = pg // 2
        maps.append(dict(q=np.ascontiguousarray(QKV[b][pg * 512:(pg + 1) * 512]),
                         k=np.ascontiguousarray(QKV[b][2048 + j * 256:2048 + (j + 1) * 256]),
                         v=np.ascontiguousarray(QKV[b][2560 + j * 256:2560 + (j + 1) * 256]),
                         idb=idb, masks=masks, lamq=lamq, lamk=lamk, subw=subw,
                         onesr=ones_r, onescb=ones_cb))
    r = run_bass_kernel_spmd(nc, maps, core_ids=list(range(NCORES)))
    AT = np.zeros((2, D, T), np.float32)
    for c in range(NCORES):
        b, pg = c // 4, c % 4
        AT[b][pg * 512:(pg + 1) * 512] = np.asarray(r.results[c]["at"])

    nc = build_l8()
    OPWst = stage_lhsT(np.ascontiguousarray(inp["L1_attn_out_proj_w"].T))
    FC1st = stage_lhsT(np.ascontiguousarray(inp["L1_attn_fc1_w"].T))
    FC2st = stage_lhsT(np.ascontiguousarray(inp["L1_attn_fc2_w"].T))
    RWg = np.ascontiguousarray(
        inp["L1_ffn_router_w"].T.reshape(16, 128, 16).transpose(1, 0, 2).reshape(128, 256))
    gu = inp["L1_ffn_gate_up_proj"]
    dnw = inp["L1_ffn_down_proj"]
    sgu = np.concatenate([inp["L1_ffn_shared_gate_w"], inp["L1_ffn_shared_up_w"]], 0)
    EGUst = np.concatenate([stage_lhsT(np.ascontiguousarray(gu[e].T)) for e in range(16)]
                           + [stage_lhsT(np.ascontiguousarray(sgu.T))], axis=1)
    EDNst = np.concatenate([stage_lhsT(np.ascontiguousarray(dnw[e].T)) for e in range(16)]
                           + [stage_lhsT(np.ascontiguousarray(inp["L1_ffn_shared_down_w"].T))],
                           axis=1)
    OPBg = np.ascontiguousarray(inp["L1_attn_out_proj_b"].reshape(16, 128).T)
    MLNWg = np.ascontiguousarray(inp["L1_attn_mlp_norm_w"].reshape(16, 128).T)
    maps = []
    for c in range(NCORES):
        b, q = c // 4, c % 4
        sl = slice(q * TC, (q + 1) * TC)
        maps.append(dict(h=np.ascontiguousarray(H[b][:, sl]),
                         at=np.ascontiguousarray(AT[b][:, sl]),
                         x2=np.ascontiguousarray(X2[b][:, sl]),
                         opw=OPWst, opb=OPBg, mlnw=MLNWg, fc1=FC1st, fc2=FC2st,
                         rw=RWg, ebr=np.ascontiguousarray(inp["L1_ffn_e_bias"].reshape(1, 16)),
                         egu=EGUst, edn=EDNst,
                         idf=np.eye(128, dtype=np.float32),
                         onesc=ones_c, onesr=ones_r))
    r = run_bass_kernel_spmd(nc, maps, core_ids=list(range(NCORES)))
    X4 = _gather_fm(r.results, "x4", D)
    return np.ascontiguousarray(X4.transpose(0, 2, 1))


DQN = 3 * 5632 * 2048
DQC = DQN // NCORES
DQW = DQC // 128

def build_l0d():
    nc = bass.Bass("TRN2", target_bir_lowering=False, debug=False)
    WQ = nc.dram_tensor("wq", [128, DQW], I32, kind="ExternalInput").ap()
    DQ = nc.dram_tensor("dq", [128, DQW], BF16, kind="ExternalOutput").ap()
    s = Sched(nc)
    ctx = ExitStack()
    with ctx:
        half = DQW // 2
        wq_sb = ctx.enter_context(nc.sbuf_tensor("wq_sb", [128, half], I32))
        dq_sb = ctx.enter_context(nc.sbuf_tensor("dq_sb", [128, half], BF16))
        for h in range(2):
            ld = s.dma("sync", wq_sb[:, :], WQ[:, h * half:(h + 1) * half])
            cv = s.add("vector", lambda e: e.tensor_scalar(
                dq_sb[:, :], wq_sb[:, :], -128.0, None, OP.add), deps=[ld])
            s.dma("gpsimd", DQ[:, h * half:(h + 1) * half], dq_sb[:, :], deps=[cv])
            s.barrier()
        s.emit(ctx)
    return nc


def build_l1():
    nc = bass.Bass("TRN2", target_bir_lowering=False, debug=False)
    XT = nc.dram_tensor("xt", [D, TC], F32R, kind="ExternalInput").ap()
    W = {p: nc.dram_tensor(f"w{p}", [128, 16 * 16 * 128], F32R, kind="ExternalInput").ap()
         for p in "fig"}
    A_out = nc.dram_tensor("a", [D, TC], F32, kind="ExternalOutput").ap()
    I_out = nc.dram_tensor("i", [D, TC], F32, kind="ExternalOutput").ap()
    GS_out = nc.dram_tensor("gs", [D, TC], F32, kind="ExternalOutput").ap()
    s = Sched(nc)
    ctx = ExitStack()
    with ctx:
        x_r = ctx.enter_context(nc.sbuf_tensor("x_r", [128, 16 * TC], F32R))
        wb = [ctx.enter_context(nc.sbuf_tensor(f"wb{j}", [128, 2048], F32R)) for j in range(2)]
        a_sb = ctx.enter_context(nc.sbuf_tensor("a_sb", [128, 16 * 512], F32))
        sn_sb = ctx.enter_context(nc.sbuf_tensor("sn_sb", [128, 16 * 512], F32))
        si_sb = ctx.enter_context(nc.sbuf_tensor("si_sb", [128, 16 * 512], F32))
        ps = [ctx.enter_context(nc.psum_tensor(f"ps{j}", [128, 512], F32)) for j in range(4)]
        load_fm(s, x_r, XT, 16, TC)
        s.barrier()
        for tc in range(2):
            t0 = tc * 512
            def evf(ob, p, stop, tt):
                e1 = s.add("scalar", lambda e, ob=ob, p=p: e.activation(
                    a_sb[:, ob * 512:(ob + 1) * 512], p[0:128, 0:512], AF.Sigmoid),
                    deps=[stop])
                return s.add("scalar", lambda e, ob=ob, p=p: e.activation(
                    sn_sb[:, ob * 512:(ob + 1) * 512], p[0:128, 0:512], AF.Sigmoid,
                    scale=-1.0), deps=[e1])
            proj(s, nc, W["f"], wb, x_r, 16, TC, ps, evf, range(16), t0=t0, t1=t0 + 512)
            s.barrier()
            for ob in range(16):
                s.dma("gpsimd", A_out[ob * 128:(ob + 1) * 128, t0:t0 + 512],
                      a_sb[:, ob * 512:(ob + 1) * 512])
            def evi(ob, p, stop, tt):
                return s.add("scalar", lambda e, ob=ob, p=p: e.activation(
                    si_sb[:, ob * 512:(ob + 1) * 512], p[0:128, 0:512], AF.Silu),
                    deps=[stop])
            proj(s, nc, W["i"], wb, x_r, 16, TC, ps, evi, range(16), t0=t0, t1=t0 + 512)
            s.barrier()
            itt = s.add("vector", lambda e: e.tensor_tensor(
                a_sb[:, :], si_sb[:, :], sn_sb[:, :], OP.mult))
            s.barrier()
            for ob in range(16):
                s.dma("gpsimd", I_out[ob * 128:(ob + 1) * 128, t0:t0 + 512],
                      a_sb[:, ob * 512:(ob + 1) * 512])
            def evg(ob, p, stop, tt):
                return s.add("scalar", lambda e, ob=ob, p=p: e.activation(
                    si_sb[:, ob * 512:(ob + 1) * 512], p[0:128, 0:512], AF.Silu),
                    deps=[stop])
            proj(s, nc, W["g"], wb, x_r, 16, TC, ps, evg, range(16), t0=t0, t1=t0 + 512)
            s.barrier()
            for ob in range(16):
                s.dma("gpsimd", GS_out[ob * 128:(ob + 1) * 128, t0:t0 + 512],
                      si_sb[:, ob * 512:(ob + 1) * 512])
            s.barrier()
        s.emit(ctx)
    return nc


def build_l2():
    nc = bass.Bass("TRN2", target_bir_lowering=False, debug=False)
    CH = D // 4
    A_in = nc.dram_tensor("a", [CH, T], F32, kind="ExternalInput").ap()
    I_in = nc.dram_tensor("i", [CH, T], F32, kind="ExternalInput").ap()
    O_out = nc.dram_tensor("o", [CH, T], F32, kind="ExternalOutput").ap()
    s = Sched(nc)
    ctx = ExitStack()
    with ctx:
        a_sb = ctx.enter_context(nc.sbuf_tensor("a_sb", [128, 4 * T], F32))
        i_sb = ctx.enter_context(nc.sbuf_tensor("i_sb", [128, 4 * T], F32))
        o_sb = ctx.enter_context(nc.sbuf_tensor("o_sb", [128, 4 * T], F32))
        for k in range(4):
            la = s.dma("sync", a_sb[:, k * T:(k + 1) * T], A_in[k * 128:(k + 1) * 128, :])
            li = s.dma("sync", i_sb[:, k * T:(k + 1) * T], I_in[k * 128:(k + 1) * 128, :])
            sc = s.add("vector", lambda e, k=k: e.tensor_tensor_scan(
                o_sb[:, k * T:(k + 1) * T], a_sb[:, k * T:(k + 1) * T],
                i_sb[:, k * T:(k + 1) * T], 0.0, OP.mult, OP.add), deps=[la, li])
            s.dma("gpsimd", O_out[k * 128:(k + 1) * 128, :], o_sb[:, k * T:(k + 1) * T],
                  deps=[sc])
        s.emit(ctx)
    return nc




def make_consts(s, nc, ctx, values):
    vals = sorted(set(float(v) for v in values))
    t = ctx.enter_context(nc.sbuf_tensor("consts", [128, max(1, len(vals))], F32)).ap()
    out = {}
    for j, v in enumerate(vals):
        s.add("vector", lambda e, j=j, v=v: e.memset(t[:, j:j + 1], v))
        out[v] = t[:, j:j + 1]
    return out

def build_l3():
    nc = bass.Bass("TRN2", target_bir_lowering=False, debug=False)
    O_in = nc.dram_tensor("o", [D, TC], F32R, kind="ExternalInput").ap()
    GS_in = nc.dram_tensor("gs", [D, TC], F32R, kind="ExternalInput").ap()
    XT_in = nc.dram_tensor("xt", [D, TC], F32R, kind="ExternalInput").ap()
    GNW = nc.dram_tensor("gnw", [128, 16], F32, kind="ExternalInput").ap()
    WO = nc.dram_tensor("wo", [128, 16 * 16 * 128], F32R, kind="ExternalInput").ap()
    DG = nc.dram_tensor("dg", [128, 44 * 16 * 128], BF16, kind="ExternalInput").ap()
    DU = nc.dram_tensor("du", [128, 44 * 16 * 128], BF16, kind="ExternalInput").ap()
    DD = nc.dram_tensor("dd", [128, 16 * 44 * 128], BF16, kind="ExternalInput").ap()
    SG = nc.dram_tensor("sg", [128, 44], F32, kind="ExternalInput").ap()
    SU = nc.dram_tensor("su", [128, 44], F32, kind="ExternalInput").ap()
    SD = nc.dram_tensor("sd", [128, 16], F32, kind="ExternalInput").ap()
    ONESC = nc.dram_tensor("onesc", [128, 1], F32R, kind="ExternalInput").ap()
    ONESR = nc.dram_tensor("onesr", [1, 128], F32R, kind="ExternalInput").ap()
    X2_out = nc.dram_tensor("x2", [D, TC], F32, kind="ExternalOutput").ap()
    if DEBUG_L3:
        DB_o2 = nc.dram_tensor("dbg_o2", [D, TC], BF16, kind="ExternalOutput").ap()
        DB_x1p = nc.dram_tensor("dbg_x1p", [D, TC], F32, kind="ExternalOutput").ap()
        DB_x1 = nc.dram_tensor("dbg_x1", [D, TC], BF16, kind="ExternalOutput").ap()
        DB_x2p = nc.dram_tensor("dbg_x2p", [D, TC], F32, kind="ExternalOutput").ap()

    s = Sched(nc)
    ctx = ExitStack()
    with ctx:
        ARF = ctx.enter_context(nc.sbuf_tensor("ARF", [128, 17920], F32R)).ap()
        ARB = ctx.enter_context(nc.sbuf_tensor("ARB", [128, 39936], BF16)).ap()
        ARM = ctx.enter_context(nc.sbuf_tensor("ARM", [128, 1024], F32R)).ap()
        OB_, OSQ, OMB, ORB = 0, 16384, 16896, 17408
        wstage = ctx.enter_context(nc.sbuf_tensor("wstage", [128, 2048], F32R)).ap()
        wo_bf = ctx.enter_context(nc.sbuf_tensor("wo_bf", [128, 2048], BF16)).ap()
        wfg = ctx.enter_context(nc.sbuf_tensor("wfg", [128, 2816], BF16)).ap()
        wfu = ctx.enter_context(nc.sbuf_tensor("wfu", [128, 2816], BF16)).ap()
        gnw = ctx.enter_context(nc.sbuf_tensor("gnwsb", [128, 16], F32)).ap()
        sg = ctx.enter_context(nc.sbuf_tensor("sgsb", [128, 44], F32)).ap()
        su = ctx.enter_context(nc.sbuf_tensor("susb", [128, 44], F32)).ap()
        sd = ctx.enter_context(nc.sbuf_tensor("sdsb", [128, 16], F32)).ap()
        onesc = ctx.enter_context(nc.sbuf_tensor("onescsb", [128, 1], F32R)).ap()
        onesr = ctx.enter_context(nc.sbuf_tensor("onesrsb", [1, 128], F32R)).ap()
        ps = [ctx.enter_context(nc.psum_tensor(f"ps{j}", [128, 512], F32)).ap() for j in range(4)]
        psr0 = ctx.enter_context(nc.psum_tensor("psr0", [1, 512], F32)).ap()
        psr1 = ctx.enter_context(nc.psum_tensor("psr1", [1, 512], F32)).ap()

        def tile(off, k, w=1024):
            return ARF[:, off + k * w: off + (k + 1) * w]

        def tile32(off, k, w=1024):
            return ARF[:, off + k * w: off + (k + 1) * w].bitcast(F32)

        o2b_all = ARB[:, 0:16384]
        x1b_all = ARB[:, 0:16384]
        prod_all = ARB[:, 16384:38912]

        def prod_slot(p):
            return prod_all[:, p * 512:(p + 1) * 512]

        sgt = ARB[:, 38912:39424]
        upt = ARB[:, 39424:39936]
        sq = ARF[:, OSQ:OSQ + 512]
        mbt = ARF[:, OMB:OMB + 512]
        rbt = ARF[:, ORB:ORB + 512]
        roll = ARM[:, 0:512]
        upc = ARM[:, 0:512]
        r2 = ARM[0:1, 0:512].bitcast(F32)
        r3 = ARM[0:1, 512:1024].bitcast(F32)
        rr0 = ARF[0:1, OSQ:OSQ + 512]
        rr1 = ARF[0:1, OMB:OMB + 512]

        for (d_, src_) in [(gnw, GNW), (sg, SG), (su, SU), (sd, SD),
                           (onesc, ONESC), (onesr, ONESR)]:
            s.dma("sync", d_[:, :], src_[:, :])
        CON = make_consts(s, nc, ctx, [EPS, 1.0 / D, -0.5])
        load_fm(s, ARF[:, OB_:OB_ + 16384], O_in, 16, TC)
        s.barrier()

        def stats_half(off, a, with_mean):
            m_sum = None
            m_sq = None
            prev_mm = None
            for i in range(16):
                if with_mean:
                    m_sum = s.add("tensor", lambda e, i=i: e.matmul(
                        psr0[0:1, 0:512], onesc[:, 0:1], tile(off, i)[:, a:a + 512],
                        start=(i == 0), stop=(i == 15)))
                sqs = s.add("scalar", lambda e, i=i: e.activation(
                    sq[:, 0:512], tile32(off, i)[:, a:a + 512], AF.Square),
                    deps=[prev_mm])
                m_sq = s.add("tensor", lambda e, i=i: e.matmul(
                    psr1[0:1, 0:512], onesc[:, 0:1], sq[:, 0:512],
                    start=(i == 0), stop=(i == 15)), deps=[sqs])
                prev_mm = m_sq
            return m_sum, m_sq

        def rows_rms(stop_sq, inv_n):
            t1 = s.add("scalar", lambda e: e.activation(
                r2[0:1, 0:512], psr1[0:1, 0:512], AF.Ln,
                scale=CON[1.0 / D][0:1, :], bias=CON[EPS][0:1, :]), deps=[stop_sq])
            return s.add("scalar", lambda e: e.activation(
                rr0[0:1, 0:512], r2[0:1, 0:512], AF.Exp, scale=CON[-0.5][0:1, :]),
                deps=[t1])

        def rows_ln(stop_sum, stop_sq, inv_n):
            m2 = s.add("scalar", lambda e: e.activation(
                r2[0:1, 0:512], psr0[0:1, 0:512], AF.Square, scale=CON[1.0 / D][0:1, :]),
                deps=[stop_sum])
            var = s.add("vector", lambda e: e.scalar_tensor_tensor(
                r3[0:1, 0:512], psr1[0:1, 0:512], float(inv_n), r2[0:1, 0:512],
                OP.mult, OP.subtract), deps=[m2, stop_sq])
            t1 = s.add("scalar", lambda e: e.activation(
                r2[0:1, 0:512], r3[0:1, 0:512], AF.Ln, bias=CON[EPS][0:1, :]), deps=[var])
            t2 = s.add("scalar", lambda e: e.activation(
                rr0[0:1, 0:512], r2[0:1, 0:512], AF.Exp, scale=CON[-0.5][0:1, :]), deps=[t1])
            t3 = s.add("vector", lambda e: e.scalar_tensor_tensor(
                rr1[0:1, 0:512], psr0[0:1, 0:512], float(-inv_n),
                rr0[0:1, 0:512].bitcast(F32), OP.mult, OP.mult), deps=[t2])
            return t3

        for tch in range(2):
            a = tch * 512
            _, sqs = stats_half(OB_, a, with_mean=False)
            rs = rows_rms(sqs, 1.0 / D)
            bcast_row(s, nc, rr0, onesr, ps[3], rbt, 512, deps=[rs])
            s.barrier()
            prev_t1 = None
            prev_t2 = None
            for i in range(16):
                lg = s.dma("sync", roll[:, 0:512], GS_in[i * 128:(i + 1) * 128, a:a + 512],
                           deps=[d for d in [prev_t1] if d])
                t1 = s.add("vector", lambda e, i=i, a=a: e.scalar_tensor_tensor(
                    sq[:, 0:512], tile32(OB_, i)[:, a:a + 512], gnw[:, i:i + 1],
                    roll[:, 0:512].bitcast(F32), OP.mult, OP.mult),
                    deps=[lg, prev_t2])
                t2 = s.add("vector", lambda e, i=i, a=a: e.tensor_tensor(
                    o2b_all[:, i * 1024 + a:i * 1024 + a + 512], sq[:, 0:512].bitcast(F32),
                    rbt.bitcast(F32), OP.mult), deps=[t1])
                prev_t1, prev_t2 = t1, t2
            s.barrier()

        if DEBUG_L3:
            for i in range(16):
                s.dma("gpsimd", DB_o2[i * 128:(i + 1) * 128, :],
                      o2b_all[:, i * 1024:(i + 1) * 1024])
            s.barrier()
        prev_mm = None
        prev_cvt = None
        prev_ev = {}
        roll_use = None
        for ob in range(16):
            lw = s.dma("sync", wstage[:, :], WO[:, ob * 2048:(ob + 1) * 2048],
                       deps=[d for d in [prev_cvt] if d])
            cvt = s.add("vector", lambda e: e.tensor_copy(wo_bf[:, :], wstage[:, :]),
                        deps=[lw] + [d for d in [prev_mm] if d])
            prev_cvt = cvt
            for ci in range(2):
                ta = ci * 512
                pps = ps[ci]
                mm = None
                for kt in range(16):
                    mm = s.add("tensor", lambda e, kt=kt, ta=ta, pps=pps: e.matmul(
                        pps[0:128, 0:512], wo_bf[:, kt * 128:(kt + 1) * 128],
                        o2b_all[:, kt * 1024 + ta:kt * 1024 + ta + 512],
                        start=(kt == 0), stop=(kt == 15)),
                        deps=([cvt, prev_ev.get(ci)] if kt == 0 else []))
                lx = s.dma("gpsimd", roll[:, 0:512], XT_in[ob * 128:(ob + 1) * 128, ta:ta + 512],
                           deps=[d for d in [roll_use] if d])
                ev = s.add("vector", lambda e, ob=ob, ta=ta, pps=pps: e.tensor_tensor(
                    tile(OB_, ob)[:, ta:ta + 512], pps[0:128, 0:512],
                    roll[:, 0:512].bitcast(F32), OP.add), deps=[mm, lx])
                prev_ev[ci] = ev
                roll_use = ev
                prev_mm = mm
        s.barrier()

        if DEBUG_L3:
            for i in range(16):
                s.dma("gpsimd", DB_x1p[i * 128:(i + 1) * 128, :],
                      tile32(OB_, i))
            s.barrier()
        for tch in range(2):
            a = tch * 512
            ssum, ssq = stats_half(OB_, a, with_mean=True)
            rw = rows_ln(ssum, ssq, 1.0 / D)
            bcast_row(s, nc, rr0, onesr, ps[3], rbt, 512, deps=[rw])
            bcast_row(s, nc, rr1, onesr, ps[2], mbt, 512, deps=[rw])
            s.barrier()
            prev = None
            for i in range(16):
                t1 = s.add("vector", lambda e, i=i, a=a: e.tensor_tensor(
                    sq[:, 0:512], tile32(OB_, i)[:, a:a + 512],
                    rbt.bitcast(F32), OP.mult), deps=[prev])
                prev = s.add("vector", lambda e, i=i, a=a: e.tensor_tensor(
                    x1b_all[:, i * 1024 + a:i * 1024 + a + 512],
                    sq[:, 0:512].bitcast(F32), mbt.bitcast(F32), OP.add), deps=[t1])
            s.barrier()

        if DEBUG_L3:
            for i in range(16):
                s.dma("gpsimd", DB_x1[i * 128:(i + 1) * 128, :],
                      x1b_all[:, i * 1024:(i + 1) * 1024])
            s.barrier()
        for tch in range(2):
            a = tch * 512
            prev_prod = None
            prev_mm_g = None
            prev_mm_u = None
            pe_g = None
            pe_u = None
            for hb in range(44):
                ldg = s.dma("sync", wfg[:, 0:2048], DG[:, hb * 2048:(hb + 1) * 2048],
                            deps=[d for d in [prev_mm_g] if d])
                ldu = s.dma("sync", wfu[:, 0:2048], DU[:, hb * 2048:(hb + 1) * 2048],
                            deps=[d for d in [prev_mm_u] if d])
                mmg = None
                mmu = None
                for kt in range(16):
                    mmg = s.add("tensor", lambda e, kt=kt, a=a: e.matmul(
                        ps[0][0:128, 0:512], wfg[:, kt * 128:(kt + 1) * 128],
                        x1b_all[:, kt * 1024 + a:kt * 1024 + a + 512],
                        start=(kt == 0), stop=(kt == 15)),
                        deps=([ldg, pe_g] if kt == 0 else []))
                for kt in range(16):
                    mmu = s.add("tensor", lambda e, kt=kt, a=a: e.matmul(
                        ps[1][0:128, 0:512], wfu[:, kt * 128:(kt + 1) * 128],
                        x1b_all[:, kt * 1024 + a:kt * 1024 + a + 512],
                        start=(kt == 0), stop=(kt == 15)),
                        deps=([ldu, pe_u] if kt == 0 else []))
                pe_g = s.add("scalar", lambda e, hb=hb: e.activation(
                    sgt[:, 0:512], ps[0][0:128, 0:512], AF.Silu,
                    scale=sg[:, hb:hb + 1]), deps=[mmg, prev_prod])
                pe_u = s.add("scalar", lambda e, hb=hb: e.activation(
                    upt[:, 0:512], ps[1][0:128, 0:512], AF.Copy,
                    scale=su[:, hb:hb + 1]), deps=[mmu, prev_prod])
                prev_prod = s.add("vector", lambda e, hb=hb: e.tensor_tensor(
                    prod_slot(hb), sgt[:, 0:512], upt[:, 0:512], OP.mult),
                    deps=[pe_g, pe_u])
                prev_mm_g, prev_mm_u = mmg, mmu
            s.barrier()
            prev_ld = None
            prev_ev = {}
            prev_cu = None
            for od in range(16):
                base = od * 5632
                ld0 = s.dma("sync", wfg[:, 0:2816], DD[:, base:base + 2816],
                            deps=[d for d in [prev_ld] if d])
                ld1 = s.dma("sync", wfu[:, 0:2816], DD[:, base + 2816:base + 5632],
                            deps=[d for d in [prev_ld] if d])
                pps = ps[2 + od % 2]
                mm = None
                for kt in range(44):
                    w_ap = wfg[:, kt * 128:(kt + 1) * 128] if kt < 22 else \
                        wfu[:, (kt - 22) * 128:(kt - 21) * 128]
                    mm = s.add("tensor", lambda e, kt=kt, w_ap=w_ap, pps=pps: e.matmul(
                        pps[0:128, 0:512], w_ap, prod_slot(kt),
                        start=(kt == 0), stop=(kt == 43)),
                        deps=([ld0, ld1, prev_ev.get(od % 2)] if kt == 0 else []))
                prev_ld = mm
                cu = s.add("vector", lambda e, od=od, a=a: e.tensor_copy(
                    upc[:, 0:512], x1b_all[:, od * 1024 + a:od * 1024 + a + 512]),
                    deps=[d for d in [prev_cu] if d])
                ev = s.add("vector", lambda e, od=od, pps=pps, a=a: e.scalar_tensor_tensor(
                    tile(OB_, od)[:, a:a + 512], pps[0:128, 0:512], sd[:, od:od + 1],
                    upc[:, 0:512].bitcast(F32), OP.mult, OP.add), deps=[mm, cu])
                prev_ev[od % 2] = ev
                prev_cu = ev
            s.barrier()

        if DEBUG_L3:
            for i in range(16):
                s.dma("gpsimd", DB_x2p[i * 128:(i + 1) * 128, :],
                      tile32(OB_, i))
            s.barrier()
        for tch in range(2):
            a = tch * 512
            ssum, ssq = stats_half(OB_, a, with_mean=True)
            rw = rows_ln(ssum, ssq, 1.0 / D)
            bcast_row(s, nc, rr0, onesr, ps[3], rbt, 512, deps=[rw])
            bcast_row(s, nc, rr1, onesr, ps[2], mbt, 512, deps=[rw])
            s.barrier()
            prev_dma = None
            prev_t1 = None
            for i in range(16):
                t1 = s.add("vector", lambda e, i=i, a=a: e.tensor_tensor(
                    sq[:, 0:512], tile32(OB_, i)[:, a:a + 512],
                    rbt.bitcast(F32), OP.mult), deps=[d for d in [prev_t1] if d])
                t2 = s.add("vector", lambda e, i=i, a=a: e.tensor_tensor(
                    roll[:, 0:512].bitcast(F32), sq[:, 0:512].bitcast(F32),
                    mbt.bitcast(F32), OP.add),
                    deps=[t1] + [d for d in [prev_dma] if d])
                prev_dma = s.dma("gpsimd", X2_out[i * 128:(i + 1) * 128, a:a + 512],
                                 roll[:, 0:512].bitcast(F32), deps=[t2])
                prev_t1 = t2
            s.barrier()
        s.emit(ctx)
    return nc


TH = TC + 4
L4STOP = 99

def build_l4():
    nc = bass.Bass("TRN2", target_bir_lowering=False, debug=False)
    X2H = nc.dram_tensor("x2h", [D, TH], F32R, kind="ExternalInput").ap()
    MNW = nc.dram_tensor("mnw", [128, 16], F32, kind="ExternalInput").ap()
    WIP = nc.dram_tensor("wip", [128, 64 * 16 * 128], F32R, kind="ExternalInput").ap()
    CONVW = nc.dram_tensor("convw", [128, 128], F32, kind="ExternalInput").ap()
    CONVB = nc.dram_tensor("convb", [128, 32], F32, kind="ExternalInput").ap()
    XPW = nc.dram_tensor("xpw", [128, 2 * 32 * 128], F32R, kind="ExternalInput").ap()
    DTW = nc.dram_tensor("dtw", [128, 32 * 128], F32R, kind="ExternalInput").ap()
    DTB = nc.dram_tensor("dtb", [128, 32], F32, kind="ExternalInput").ap()
    ONESC = nc.dram_tensor("onesc", [128, 1], F32R, kind="ExternalInput").ap()
    ONESR = nc.dram_tensor("onesr", [1, 128], F32R, kind="ExternalInput").ap()
    XS_out = nc.dram_tensor("xs", [4096, TC], F32, kind="ExternalOutput").ap()
    DT_out = nc.dram_tensor("dt", [4096, TC], F32, kind="ExternalOutput").ap()
    BC_out = nc.dram_tensor("bc", [32, TC], F32, kind="ExternalOutput").ap()
    ZS_out = nc.dram_tensor("zs", [4096, TC], F32, kind="ExternalOutput").ap()

    s = Sched(nc)
    ctx = ExitStack()
    with ctx:
        HNF = ctx.enter_context(nc.sbuf_tensor("HNF", [128, 16 * TH], F32R)).ap()
        XINF = ctx.enter_context(nc.sbuf_tensor("XINF", [128, 32 * 260], F32R)).ap()
        XCF = ctx.enter_context(nc.sbuf_tensor("XCF", [128, 32 * 256], F32R)).ap()
        SCR = ctx.enter_context(nc.sbuf_tensor("SCR", [128, 2048], F32R)).ap()
        MISC = ctx.enter_context(nc.sbuf_tensor("MISC", [128, 2304], F32)).ap()
        WB = [ctx.enter_context(nc.sbuf_tensor(f"WB{j}", [128, 2048], F32R)).ap() for j in range(2)]
        XPB = ctx.enter_context(nc.sbuf_tensor("XPB", [128, 4096], F32R)).ap()
        DTWB = ctx.enter_context(nc.sbuf_tensor("DTWB", [128, 4096], F32R)).ap()
        mnw = ctx.enter_context(nc.sbuf_tensor("mnwsb", [128, 16], F32)).ap()
        cvw = ctx.enter_context(nc.sbuf_tensor("cvwsb", [128, 128], F32)).ap()
        cvb = ctx.enter_context(nc.sbuf_tensor("cvbsb", [128, 32], F32)).ap()
        dtb = ctx.enter_context(nc.sbuf_tensor("dtbsb", [128, 32], F32)).ap()
        onesc = ctx.enter_context(nc.sbuf_tensor("onescsb", [128, 1], F32R)).ap()
        onesr = ctx.enter_context(nc.sbuf_tensor("onesrsb", [1, 128], F32R)).ap()
        ps = [ctx.enter_context(nc.psum_tensor(f"ps{j}", [128, 512], F32)).ap() for j in range(4)]
        psr1 = ctx.enter_context(nc.psum_tensor("psr1", [1, 512], F32)).ap()

        sq = SCR[:, 0:512]
        rbt = SCR[:, 512:1024]
        acc = [SCR[:, 1024:1280], SCR[:, 1280:1536]]
        dtin = SCR[:, 1536:1792]
        bc = SCR[:, 1792:2048]
        rr0 = SCR[0:1, 0:512]
        zsl = [MISC[:, 0:256], MISC[:, 256:512]]
        sp1 = MISC[:, 512:768]
        sp2 = MISC[:, 768:1024]
        sp3 = MISC[:, 1024:1280]
        r2 = MISC[0:1, 1280:1792]

        def htile(k):
            return HNF[:, k * TH:(k + 1) * TH]

        for (d_, src_) in [(mnw, MNW), (cvw, CONVW), (cvb, CONVB), (dtb, DTB),
                           (onesc, ONESC), (onesr, ONESR)]:
            s.dma("sync", d_[:, :], src_[:, :])
        CON = make_consts(s, nc, ctx, [EPS, 1.0 / D, -0.5, -1.0])
        load_fm(s, HNF, X2H, 16, TH)
        s.dma("gpsimd", XPB[:, :], XPW[:, 0:4096])
        s.dma("gpsimd", DTWB[:, :], DTW[:, :])
        s.barrier()

        for (a, w) in ((0, 512), (512, 512), (1024, 4)):
            prev_mm = None
            m_sq = None
            for i in range(16):
                sqs = s.add("scalar", lambda e, i=i, a=a, w=w: e.activation(
                    sq[:, 0:w], htile(i).bitcast(F32)[:, a:a + w], AF.Square),
                    deps=[prev_mm])
                m_sq = s.add("tensor", lambda e, i=i, w=w: e.matmul(
                    psr1[0:1, 0:w], onesc[:, 0:1], sq[:, 0:w],
                    start=(i == 0), stop=(i == 15)), deps=[sqs])
                prev_mm = m_sq
            t1 = s.add("scalar", lambda e, w=w: e.activation(
                r2[0:1, 0:w], psr1[0:1, 0:w], AF.Ln,
                scale=CON[1.0 / D][0:1, :], bias=CON[EPS][0:1, :]), deps=[m_sq])
            t2 = s.add("scalar", lambda e, w=w: e.activation(
                rr0[0:1, 0:w], r2[0:1, 0:w], AF.Exp, scale=CON[-0.5][0:1, :]),
                deps=[t1])
            bcast_row(s, nc, rr0, onesr, ps[3], rbt, w, deps=[t2])
            s.barrier()
            for i in range(16):
                s.add("vector", lambda e, i=i, a=a, w=w: e.scalar_tensor_tensor(
                    htile(i)[:, a:a + w], htile(i).bitcast(F32)[:, a:a + w],
                    mnw[:, i:i + 1], rbt[:, 0:w].bitcast(F32), OP.mult, OP.mult))
            s.barrier()

        for c in range(4 if L4STOP > 1 else 0):
            t0 = c * 256
            prev_mm = {}
            prev_ev = {}
            zuse = {}
            for ob in range(64):
                buf = ob % 2
                ld = s.dma("sync", WB[buf][:, :], WIP[:, ob * 2048:(ob + 1) * 2048],
                           deps=[d for d in [prev_mm.get(buf)] if d])
                pps = ps[ob % 2]
                mm = None
                if ob < 32:
                    ca, cw = t0, 260
                else:
                    ca, cw = t0 + 4, 256
                for kt in range(16):
                    mm = s.add("tensor", lambda e, kt=kt, ca=ca, cw=cw, pps=pps, buf=buf: e.matmul(
                        pps[0:128, 0:cw], WB[buf][:, kt * 128:(kt + 1) * 128],
                        HNF[:, kt * TH + ca:kt * TH + ca + cw],
                        start=(kt == 0), stop=(kt == 15)),
                        deps=([ld, prev_ev.get(ob % 2)] if kt == 0 else []))
                prev_mm[buf] = mm
                if ob < 32:
                    ev = s.add("vector", lambda e, ob=ob, pps=pps: e.tensor_copy(
                        XINF[:, ob * 260:(ob + 1) * 260], pps[0:128, 0:260]), deps=[mm])
                else:
                    zs_ = zsl[ob % 2]
                    ev = s.add("scalar", lambda e, pps=pps, zs_=zs_: e.activation(
                        zs_[:, 0:256], pps[0:128, 0:256], AF.Silu),
                        deps=[mm] + [d for d in [zuse.get(ob % 2)] if d])
                    zuse[ob % 2] = s.dma("gpsimd",
                                         ZS_out[(ob - 32) * 128:(ob - 31) * 128, t0:t0 + 256],
                                         zs_[:, 0:256], deps=[ev])
                prev_ev[ob % 2] = ev
            s.barrier()
            if L4STOP < 3:
                continue
            prev = {}
            for dt_ in range(32):
                a0 = acc[dt_ % 2]
                c1 = s.add("vector", lambda e, dt_=dt_, a0=a0: e.tensor_scalar(
                    a0[:, 0:256], XINF.bitcast(F32)[:, dt_ * 260 + 1:dt_ * 260 + 257],
                    cvw[:, dt_ * 4:dt_ * 4 + 1], None, OP.mult),
                    deps=[d for d in [prev.get(dt_ % 2)] if d])
                for i in (1, 2, 3):
                    c1 = s.add("vector", lambda e, dt_=dt_, i=i, a0=a0: e.scalar_tensor_tensor(
                        a0[:, 0:256], XINF.bitcast(F32)[:, dt_ * 260 + 1 + i:dt_ * 260 + 257 + i],
                        cvw[:, dt_ * 4 + i:dt_ * 4 + i + 1], a0[:, 0:256].bitcast(F32),
                        OP.mult, OP.add), deps=[c1])
                sl = s.add("scalar", lambda e, dt_=dt_, a0=a0: e.activation(
                    XCF[:, dt_ * 256:(dt_ + 1) * 256], a0[:, 0:256].bitcast(F32),
                    AF.Silu, bias=cvb[:, dt_:dt_ + 1]), deps=[c1])
                prev[dt_ % 2] = sl
                s.dma("gpsimd", XS_out[dt_ * 128:(dt_ + 1) * 128, t0:t0 + 256],
                      XCF.bitcast(F32)[:, dt_ * 256:(dt_ + 1) * 256], deps=[sl])
            s.barrier()
            if L4STOP < 4:
                continue
            for ob2 in range(2):
                if ob2 == 1:
                    lx = s.dma("sync", XPB[:, :], XPW[:, 4096:8192])
                else:
                    lx = None
                mm = None
                for kt in range(32):
                    mm = s.add("tensor", lambda e, kt=kt, ob2=ob2: e.matmul(
                        ps[2][0:128, 0:256], XPB[:, kt * 128:(kt + 1) * 128],
                        XCF[:, kt * 256:(kt + 1) * 256],
                        start=(kt == 0), stop=(kt == 31)),
                        deps=([lx] if (kt == 0 and lx) else []))
                tgt = dtin if ob2 == 0 else bc
                ev = s.add("vector", lambda e, tgt=tgt: e.tensor_copy(
                    tgt[:, 0:256], ps[2][0:128, 0:256]), deps=[mm])
                s.barrier()
            s.dma("gpsimd", BC_out[0:32, t0:t0 + 256], bc.bitcast(F32)[0:32, 0:256])
            if c < 3:
                s.dma("sync", XPB[:, :], XPW[:, 0:4096])
            if L4STOP < 5:
                continue
            prev_dma = {}
            prev_m1 = {}
            prev_dd = None
            for ob in range(32):
                pps = ps[ob % 2]
                mm = s.add("tensor", lambda e, ob=ob, pps=pps: e.matmul(
                    pps[0:128, 0:256], DTWB[:, ob * 128:(ob + 1) * 128],
                    dtin[:, 0:256], start=True, stop=True),
                    deps=[d for d in [prev_m1.get(ob % 2)] if d])
                a1 = s.add("scalar", lambda e, ob=ob, pps=pps: e.activation(
                    sp1[:, 0:256], pps[0:128, 0:256], AF.Abs, bias=dtb[:, ob:ob + 1]),
                    deps=[mm] + [d for d in [prev_dd] if d])
                a2 = s.add("scalar", lambda e: e.activation(
                    sp2[:, 0:256], sp1[:, 0:256], AF.Exp, scale=CON[-1.0]), deps=[a1])
                a3 = s.add("scalar", lambda e: e.activation(
                    sp1[:, 0:256], sp2[:, 0:256], AF.Ln, bias=1.0), deps=[a2])
                m1 = s.add("scalar", lambda e, ob=ob, pps=pps: e.activation(
                    sp2[:, 0:256], pps[0:128, 0:256], AF.Relu, bias=dtb[:, ob:ob + 1]),
                    deps=[a3])
                dd = s.add("vector", lambda e, ob=ob: e.tensor_tensor(
                    sp3[:, 0:256], sp1[:, 0:256], sp2[:, 0:256], OP.add),
                    deps=[m1] + [d for d in [prev_dma.get(0)] if d])
                prev_dd = dd
                prev_m1[ob % 2] = m1
                prev_dma[0] = s.dma("gpsimd", DT_out[ob * 128:(ob + 1) * 128, t0:t0 + 256],
                                    sp3[:, 0:256], deps=[dd])
            s.barrier()
        s.emit(ctx)
    return nc


def build_l5():
    nc = bass.Bass("TRN2", target_bir_lowering=False, debug=False)
    XS = nc.dram_tensor("xs", [1024, T], F32, kind="ExternalInput").ap()
    DTT = nc.dram_tensor("dt", [1024, T], F32, kind="ExternalInput").ap()
    BC = nc.dram_tensor("bc", [32, T], F32, kind="ExternalInput").ap()
    ALOG = nc.dram_tensor("alog", [128, 128], F32, kind="ExternalInput").ap()
    DP = nc.dram_tensor("dp", [128, 8], F32, kind="ExternalInput").ap()
    Y_out = nc.dram_tensor("y", [1024, T], F32, kind="ExternalOutput").ap()

    s = Sched(nc)
    ctx = ExitStack()
    with ctx:
        sb = lambda n, w, dt=F32: ctx.enter_context(nc.sbuf_tensor(n, [128, w], dt)).ap()
        xs_t = sb("xs_t", T); dt_t = sb("dt_t", T); xdt = sb("xdt", T)
        da = sb("da", T); u_t = sb("u_t", T); S_t = sb("S_t", T); y_t = sb("y_t", T)
        Bb = [sb(f"Bb{j}", T) for j in range(2)]
        Cb = [sb(f"Cb{j}", T) for j in range(2)]
        alog = sb("alog_sb", 128); nega = sb("nega", 128); dp = sb("dp_sb", 8)
        la = s.dma("sync", alog[:, :], ALOG[:, :])
        s.dma("sync", dp[:, :], DP[:, :])
        e1 = s.add("scalar", lambda e: e.activation(nega[:, :], alog[:, :], AF.Exp), deps=[la])
        s.add("vector", lambda e: e.tensor_scalar(nega[:, :], nega[:, :], -1.0, None, OP.mult),
              deps=[e1])
        s.barrier()
        for dtile in range(8):
            lx = s.dma("sync", xs_t[:, :], XS[dtile * 128:(dtile + 1) * 128, :])
            ld = s.dma("sync", dt_t[:, :], DTT[dtile * 128:(dtile + 1) * 128, :])
            tx = s.add("vector", lambda e: e.tensor_tensor(
                xdt[:, :], xs_t[:, :], dt_t[:, :], OP.mult), deps=[lx, ld])
            buse = {}
            yacc = None
            prev_sc = None
            for n in range(16):
                bb = Bb[n % 2]; cb = Cb[n % 2]
                lb = s.dma("gpsimd", bb[:, :], BC[n:n + 1, :].broadcast_to([128, T]),
                           deps=[d for d in [buse.get(n % 2)] if d])
                lc = s.dma("gpsimd", cb[:, :], BC[16 + n:17 + n, :].broadcast_to([128, T]),
                           deps=[d for d in [buse.get(n % 2)] if d])
                ea = s.add("scalar", lambda e, dtile=dtile, n=n: e.activation(
                    da[:, :], dt_t[:, :], AF.Exp,
                    scale=nega[:, dtile * 16 + n:dtile * 16 + n + 1]),
                    deps=[ld] + [d for d in [prev_sc] if d])
                uu = s.add("vector", lambda e, bb=bb: e.tensor_tensor(
                    u_t[:, :], xdt[:, :], bb[:, :], OP.mult),
                    deps=[tx, lb] + [d for d in [yacc] if d])
                sc = s.add("vector", lambda e: e.tensor_tensor_scan(
                    S_t[:, :], da[:, :], u_t[:, :], 0.0, OP.mult, OP.add),
                    deps=[ea, uu])
                prev_sc = sc
                pp = s.add("vector", lambda e, cb=cb: e.tensor_tensor(
                    u_t[:, :], S_t[:, :], cb[:, :], OP.mult), deps=[sc, lc])
                buse[n % 2] = pp
                if n == 0:
                    yacc = s.add("vector", lambda e: e.tensor_copy(y_t[:, :], u_t[:, :]),
                                 deps=[pp])
                else:
                    yacc = s.add("vector", lambda e: e.tensor_tensor(
                        y_t[:, :], y_t[:, :], u_t[:, :], OP.add), deps=[pp, yacc])
            yfin = s.add("vector", lambda e, dtile=dtile: e.scalar_tensor_tensor(
                da[:, :], xs_t[:, :], dp[:, dtile:dtile + 1], y_t[:, :],
                OP.mult, OP.add), deps=[yacc])
            s.dma("gpsimd", Y_out[dtile * 128:(dtile + 1) * 128, :], da[:, :], deps=[yfin])
            s.barrier()
        s.emit(ctx)
    return nc


def build_l6():
    nc = bass.Bass("TRN2", target_bir_lowering=False, debug=False)
    Y_in = nc.dram_tensor("y", [4096, TC], F32, kind="ExternalInput").ap()
    ZS_in = nc.dram_tensor("zs", [4096, TC], F32, kind="ExternalInput").ap()
    X2T = nc.dram_tensor("x2", [D, TC], F32, kind="ExternalInput").ap()
    ANW = nc.dram_tensor("anw", [128, 16], F32, kind="ExternalInput").ap()
    WMO = nc.dram_tensor("wmo", [128, 16 * 32 * 128], F32, kind="ExternalInput").ap()
    WQKV = nc.dram_tensor("wqkv", [128, 24 * 16 * 128], F32, kind="ExternalInput").ap()
    QKVB = nc.dram_tensor("qkvb", [128, 24], F32, kind="ExternalInput").ap()
    ONESCB = nc.dram_tensor("onescb", [128, 1], BF16, kind="ExternalInput").ap()
    ONESR = nc.dram_tensor("onesr", [1, 128], F32R, kind="ExternalInput").ap()
    H_out = nc.dram_tensor("h", [D, TC], F32, kind="ExternalOutput").ap()
    QKV_out = nc.dram_tensor("qkv", [3072, TC], F32, kind="ExternalOutput").ap()

    s = Sched(nc)
    ctx = ExitStack()
    with ctx:
        YMB = ctx.enter_context(nc.sbuf_tensor("YMB", [128, 32 * 1024], BF16)).ap()
        HF = ctx.enter_context(nc.sbuf_tensor("HF", [128, 16 * 1024], F32)).ap()
        WST = ctx.enter_context(nc.sbuf_tensor("WST", [128, 4096], F32)).ap()
        WBF = [ctx.enter_context(nc.sbuf_tensor(f"WBF{j}", [128, 4096], BF16)).ap() for j in range(2)]
        SQB = ctx.enter_context(nc.sbuf_tensor("SQB", [128, 512], BF16)).ap()
        RBT = ctx.enter_context(nc.sbuf_tensor("RBT", [128, 512], F32)).ap()
        ROLLF = ctx.enter_context(nc.sbuf_tensor("ROLLF", [128, 5120], F32)).ap()
        r2 = ctx.enter_context(nc.sbuf_tensor("r2", [1, 512], F32)).ap()
        rr0 = ctx.enter_context(nc.sbuf_tensor("rr0", [1, 512], F32R)).ap()
        anw = ctx.enter_context(nc.sbuf_tensor("anwsb", [128, 16], F32)).ap()
        qkvb = ctx.enter_context(nc.sbuf_tensor("qkvbsb", [128, 24], F32)).ap()
        onescb = ctx.enter_context(nc.sbuf_tensor("onescbsb", [128, 1], BF16)).ap()
        onesr = ctx.enter_context(nc.sbuf_tensor("onesrsb", [1, 128], F32R)).ap()
        ps = [ctx.enter_context(nc.psum_tensor(f"ps{j}", [128, 512], F32)).ap() for j in range(4)]
        psr1 = ctx.enter_context(nc.psum_tensor("psr1", [1, 512], F32)).ap()

        ra = [ROLLF[:, 0:1024], ROLLF[:, 1024:2048]]
        rb = [ROLLF[:, 2048:3072], ROLLF[:, 3072:4096]]
        rx = [ROLLF[:, 4096:4608], ROLLF[:, 4608:5120]]

        for (d_, src_) in [(anw, ANW), (qkvb, QKVB), (onescb, ONESCB), (onesr, ONESR)]:
            s.dma("sync", d_[:, :], src_[:, :])
        CON = make_consts(s, nc, ctx, [EPS, 1.0 / D, -0.5])
        s.barrier()

        use = {}
        for i in range(32):
            ly = s.dma("sync", ra[i % 2][:, :], Y_in[i * 128:(i + 1) * 128, :],
                       deps=[d for d in [use.get(i % 2)] if d])
            lz = s.dma("gpsimd", rb[i % 2][:, :], ZS_in[i * 128:(i + 1) * 128, :],
                       deps=[d for d in [use.get(i % 2)] if d])
            use[i % 2] = s.add("vector", lambda e, i=i: e.tensor_tensor(
                YMB[:, i * 1024:(i + 1) * 1024], ra[i % 2][:, :], rb[i % 2][:, :],
                OP.mult), deps=[ly, lz])
        s.barrier()

        prev_mm = {}
        prev_cvt = {}
        prev_ev = {}
        xuse = {}
        for ob in range(16):
            buf = ob % 2
            lw = s.dma("sync", WST[:, :], WMO[:, ob * 4096:(ob + 1) * 4096],
                       deps=[d for d in [prev_cvt.get(0)] if d])
            cvt = s.add("vector", lambda e, buf=buf: e.tensor_copy(WBF[buf][:, :], WST[:, :]),
                        deps=[lw] + [d for d in [prev_mm.get(buf)] if d])
            prev_cvt[0] = cvt
            for ci in range(2):
                ta = ci * 512
                pps = ps[ci]
                mm = None
                for kt in range(32):
                    mm = s.add("tensor", lambda e, kt=kt, ta=ta, pps=pps, buf=buf: e.matmul(
                        pps[0:128, 0:512], WBF[buf][:, kt * 128:(kt + 1) * 128],
                        YMB[:, kt * 1024 + ta:kt * 1024 + ta + 512],
                        start=(kt == 0), stop=(kt == 31)),
                        deps=([cvt, prev_ev.get(ci)] if kt == 0 else []))
                lx = s.dma("gpsimd", rx[ci][:, :], X2T[ob * 128:(ob + 1) * 128, ta:ta + 512],
                           deps=[d for d in [xuse.get(ci)] if d])
                ev = s.add("vector", lambda e, ob=ob, ta=ta, pps=pps, ci=ci: e.tensor_tensor(
                    HF[:, ob * 1024 + ta:ob * 1024 + ta + 512], pps[0:128, 0:512],
                    rx[ci][:, :], OP.add), deps=[mm, lx])
                xuse[ci] = ev
                prev_ev[ci] = ev
                prev_mm[buf] = mm
            s.dma("gpsimd", H_out[ob * 128:(ob + 1) * 128, :], HF[:, ob * 1024:(ob + 1) * 1024],
                  deps=[prev_ev[0], prev_ev[1]])
        s.barrier()

        for tch in range(2):
            a = tch * 512
            prev_mm2 = None
            m_sq = None
            for i in range(16):
                sqs = s.add("scalar", lambda e, i=i, a=a: e.activation(
                    SQB[:, 0:512], HF[:, i * 1024 + a:i * 1024 + a + 512], AF.Square),
                    deps=[prev_mm2])
                m_sq = s.add("tensor", lambda e, i=i: e.matmul(
                    psr1[0:1, 0:512], onescb[:, 0:1], SQB[:, 0:512],
                    start=(i == 0), stop=(i == 15)), deps=[sqs])
                prev_mm2 = m_sq
            t1 = s.add("scalar", lambda e: e.activation(
                r2[0:1, 0:512], psr1[0:1, 0:512], AF.Ln,
                scale=CON[1.0 / D][0:1, :], bias=CON[EPS][0:1, :]), deps=[m_sq])
            t2 = s.add("scalar", lambda e: e.activation(
                rr0[0:1, 0:512], r2[0:1, 0:512], AF.Exp, scale=CON[-0.5][0:1, :]), deps=[t1])
            bcast_row(s, nc, rr0, onesr, ps[3], RBT[:, 0:512], 512, deps=[t2])
            s.barrier()
            for i in range(16):
                s.add("vector", lambda e, i=i, a=a: e.scalar_tensor_tensor(
                    YMB[:, i * 1024 + a:i * 1024 + a + 512],
                    HF[:, i * 1024 + a:i * 1024 + a + 512], anw[:, i:i + 1],
                    RBT[:, 0:512], OP.mult, OP.mult))
            s.barrier()

        prev_mm = {}
        prev_cvt = {}
        prev_ev = {}
        quse = {}
        for ob in range(24):
            buf = ob % 2
            lw = s.dma("sync", WST[:, 0:2048], WQKV[:, ob * 2048:(ob + 1) * 2048],
                       deps=[d for d in [prev_cvt.get(0)] if d])
            cvt = s.add("vector", lambda e, buf=buf: e.tensor_copy(
                WBF[buf][:, 0:2048], WST[:, 0:2048]),
                deps=[lw] + [d for d in [prev_mm.get(buf)] if d])
            prev_cvt[0] = cvt
            for ci in range(2):
                ta = ci * 512
                pps = ps[ci]
                mm = None
                for kt in range(16):
                    mm = s.add("tensor", lambda e, kt=kt, ta=ta, pps=pps, buf=buf: e.matmul(
                        pps[0:128, 0:512], WBF[buf][:, kt * 128:(kt + 1) * 128],
                        YMB[:, kt * 1024 + ta:kt * 1024 + ta + 512],
                        start=(kt == 0), stop=(kt == 15)),
                        deps=([cvt, prev_ev.get(ci)] if kt == 0 else []))
                ev = s.add("vector", lambda e, ob=ob, pps=pps, ci=ci: e.tensor_scalar(
                    rx[ci][:, :], pps[0:128, 0:512], qkvb[:, ob:ob + 1], None, OP.add),
                    deps=[mm] + [d for d in [quse.get(ci)] if d])
                quse[ci] = s.dma("gpsimd", QKV_out[ob * 128:(ob + 1) * 128, ta:ta + 512],
                                 rx[ci][:, :], deps=[ev])
                prev_ev[ci] = ev
                prev_mm[buf] = mm
        s.barrier()
        s.emit(ctx)
    return nc


LAMBDA_INIT = 0.8 - 0.6 * float(np.exp(-0.3))
INVSQ = 1.0 / float(np.sqrt(128.0))

def attn_kbs(qt):
    return [kb for kb in range(0, qt // 4 + 1) if (qt - 4 * kb) <= 19]

def build_l7():
    nc = bass.Bass("TRN2", target_bir_lowering=False, debug=False)
    Q_in = nc.dram_tensor("q", [512, T], F32R, kind="ExternalInput").ap()
    K_in = nc.dram_tensor("k", [256, T], F32R, kind="ExternalInput").ap()
    V_in = nc.dram_tensor("v", [256, T], F32R, kind="ExternalInput").ap()
    IDB = nc.dram_tensor("idb", [128, 128], BF16, kind="ExternalInput").ap()
    MASKS = nc.dram_tensor("masks", [128, 8 * 512], F32, kind="ExternalInput").ap()
    LAMQ = nc.dram_tensor("lamq", [1, 256], F32, kind="ExternalInput").ap()
    LAMK = nc.dram_tensor("lamk", [1, 256], F32, kind="ExternalInput").ap()
    SUBW = nc.dram_tensor("subw", [128, 2], F32, kind="ExternalInput").ap()
    ONESR = nc.dram_tensor("onesr", [1, 128], F32R, kind="ExternalInput").ap()
    ONESCB = nc.dram_tensor("onescb", [128, 1], BF16, kind="ExternalInput").ap()
    AT_out = nc.dram_tensor("at", [512, T], F32, kind="ExternalOutput").ap()

    s = Sched(nc)
    ctx = ExitStack()
    with ctx:
        qs = [ctx.enter_context(nc.sbuf_tensor(f"qs{j}", [128, T], F32R)).ap() for j in range(4)]
        ks = [ctx.enter_context(nc.sbuf_tensor(f"ks{j}", [128, T], F32R)).ap() for j in range(2)]
        vT = [ctx.enter_context(nc.sbuf_tensor(f"vT{j}", [128, T], BF16)).ap() for j in range(2)]
        vb = ctx.enter_context(nc.sbuf_tensor("vb", [128, T], BF16)).ap()
        sS = [ctx.enter_context(nc.sbuf_tensor(f"sS{j}", [128, 2560], F32)).ap() for j in range(2)]
        pB = [ctx.enter_context(nc.sbuf_tensor(f"pB{j}", [128, 2560], BF16)).ap() for j in range(2)]
        pT = [ctx.enter_context(nc.sbuf_tensor(f"pT{j}", [128, 2560], BF16)).ap() for j in range(2)]
        msk = ctx.enter_context(nc.sbuf_tensor("msk", [128, 8 * 512], F32)).ap()
        idb = ctx.enter_context(nc.sbuf_tensor("idb_sb", [128, 128], BF16)).ap()
        subw = ctx.enter_context(nc.sbuf_tensor("subw_sb", [128, 2], F32)).ap()
        onesr = ctx.enter_context(nc.sbuf_tensor("onesr_sb", [1, 128], F32R)).ap()
        onescb = ctx.enter_context(nc.sbuf_tensor("onescb_sb", [128, 1], BF16)).ap()
        lamq = ctx.enter_context(nc.sbuf_tensor("lamq_sb", [1, 256], F32)).ap()
        lamk = ctx.enter_context(nc.sbuf_tensor("lamk_sb", [1, 256], F32)).ap()
        lrow = ctx.enter_context(nc.sbuf_tensor("lrow", [1, 256], F32)).ap()
        lr2 = ctx.enter_context(nc.sbuf_tensor("lr2", [1, 8], F32)).ap()
        nlam2 = ctx.enter_context(nc.sbuf_tensor("nlam2", [1, 2], F32R)).ap()
        nlamC = ctx.enter_context(nc.sbuf_tensor("nlamC", [128, 1], F32)).ap()
        rowm = [ctx.enter_context(nc.sbuf_tensor(f"rowm{j}", [128, 1], F32)).ap() for j in range(2)]
        negb = [ctx.enter_context(nc.sbuf_tensor(f"negb{j}", [128, 1], F32)).ap() for j in range(2)]
        rs = [ctx.enter_context(nc.sbuf_tensor(f"rs{j}", [128, 1], F32)).ap() for j in range(2)]
        rinv = [ctx.enter_context(nc.sbuf_tensor(f"rinv{j}", [128, 1], F32)).ap() for j in range(2)]
        a1sb = [ctx.enter_context(nc.sbuf_tensor(f"a1sb{j}", [128, 256], F32)).ap() for j in range(2)]
        dsb = [ctx.enter_context(nc.sbuf_tensor(f"dsb{j}", [128, 256], F32)).ap() for j in range(2)]
        sqb = [ctx.enter_context(nc.sbuf_tensor(f"sqb_t{j}", [128, 128], BF16)).ap() for j in range(2)]
        r2 = ctx.enter_context(nc.sbuf_tensor("r2", [1, 128], F32)).ap()
        rr0 = [ctx.enter_context(nc.sbuf_tensor(f"rr0_{j}", [1, 128], F32R)).ap() for j in range(2)]
        rbF = [ctx.enter_context(nc.sbuf_tensor(f"rbF{j}", [128, 128], F32)).ap() for j in range(2)]
        oF = [ctx.enter_context(nc.sbuf_tensor(f"oF{j}", [128, 256], F32)).ap() for j in range(2)]
        ps_s = [ctx.enter_context(nc.psum_tensor(f"ps_s{j}", [128, 512], F32)).ap() for j in range(2)]
        psT = [ctx.enter_context(nc.psum_tensor(f"psT{j}", [128, 128], BF16)).ap() for j in range(2)]
        psO = [ctx.enter_context(nc.psum_tensor(f"psO{j}", [128, 512], F32)).ap() for j in range(2)]
        psr = ctx.enter_context(nc.psum_tensor("psr", [1, 128], F32)).ap()

        CON = make_consts(s, nc, ctx, [EPS, 1.0 / 256, -0.5, INVSQ,
                                       float(np.log(1.0 - LAMBDA_INIT))])
        for j in range(4):
            s.dma("sync", qs[j][:, :], Q_in[j * 128:(j + 1) * 128, :])
        for j in range(2):
            s.dma("sync", ks[j][:, :], K_in[j * 128:(j + 1) * 128, :])
        for (d_, src_) in [(msk, MASKS), (idb, IDB), (subw, SUBW), (onesr, ONESR),
                           (onescb, ONESCB), (lamq, LAMQ), (lamk, LAMK)]:
            s.dma("gpsimd", d_[:, :], src_[:, :])
        s.barrier()

        t1 = s.add("vector", lambda e: e.tensor_tensor(lrow[0:1, :], lamq[0:1, :],
                                                       lamk[0:1, :], OP.mult))
        t2 = s.add("vector", lambda e: e.tensor_reduce(
            lr2[0:1, 0:1], lrow[0:1, 0:128], mb.AxisListType.X, OP.add), deps=[t1])
        t3 = s.add("vector", lambda e: e.tensor_reduce(
            lr2[0:1, 1:2], lrow[0:1, 128:256], mb.AxisListType.X, OP.add), deps=[t1])
        t4 = s.add("scalar", lambda e: e.activation(lr2[0:1, 2:4], lr2[0:1, 0:2], AF.Exp),
                   deps=[t2, t3])
        t5 = s.add("vector", lambda e: e.tensor_tensor(
            lr2[0:1, 4:5], lr2[0:1, 3:4], lr2[0:1, 2:3], OP.subtract), deps=[t4])
        t6 = s.add("vector", lambda e: e.tensor_scalar(
            nlam2[0:1, 0:2], lr2[0:1, 4:5].broadcast_to([1, 2]), -LAMBDA_INIT, None,
            OP.add), deps=[t5])
        t7 = s.add("tensor", lambda e: e.matmul(
            psO[0][0:128, 0:2], onesr[0:1, :], nlam2[0:1, 0:2], start=True, stop=True),
            deps=[t6])
        t8 = s.add("vector", lambda e: e.tensor_copy(nlamC[:, 0:1], psO[0][0:128, 0:1]),
                   deps=[t7])
        prev_ev = None
        for j in range(2):
            cv = s.dma("gpsimd", vb[:, :], V_in[j * 128:(j + 1) * 128, :],
                       deps=[d for d in [prev_ev] if d])
            for g in range(32):
                tp = s.add("tensor", lambda e, g=g: e.transpose(
                    psT[0][:, :], vb[:, g * 128:(g + 1) * 128], idb[:, :]),
                    deps=[cv] + ([prev_ev] if prev_ev else []))
                prev_ev = s.add("vector", lambda e, j=j, g=g: e.tensor_copy(
                    vT[j][:, g * 128:(g + 1) * 128], psT[0][:, :]), deps=[tp])
        s.barrier()

        MIDX = {0: 0, 1: 1, 2: 2, 3: 3, 16: 4, 17: 5, 18: 6, 19: 7}
        sS_rd = {}
        pB_rd = {}
        pT_rd = {}
        psO_rd = {}
        rc_pr = {}
        nb_pr = {}
        dst_pr = {}
        oo_pr = {}
        msq_pr = None
        rw1_pr = None
        tev_pr = {}
        for p in range(2):
            q1 = qs[p * 2]; q2 = qs[p * 2 + 1]
            for qt in range(32):
                par = (p * 32 + qt) % 2
                kbs = attn_kbs(qt)
                nkb = len(kbs)
                kwin = nkb * 512
                last_pv = None
                for ci, qv in ((0, q1), (1, q2)):
                    evs = []
                    for j, kb in enumerate(kbs):
                        dq = qt - 4 * kb
                        mm = s.add("tensor", lambda e, qv=qv, qt=qt, kb=kb, ci=ci, par=par: e.matmul(
                            ps_s[par][0:128, 0:512], qv[:, qt * 128:(qt + 1) * 128],
                            ks[ci][:, kb * 512:(kb + 1) * 512], start=True, stop=True),
                            deps=evs[-1:] if evs else [d for d in [sS_rd.get(par)] if d])
                        if dq in MIDX and (dq <= 3 or dq >= 16):
                            mi = MIDX[dq]
                            ev = s.add("vector", lambda e, j=j, mi=mi, par=par: e.tensor_tensor(
                                sS[par][:, j * 512:(j + 1) * 512], ps_s[par][0:128, 0:512],
                                msk[:, mi * 512:(mi + 1) * 512], OP.add),
                                deps=[mm] + [d for d in [sS_rd.get(par)] if d])
                        else:
                            ev = s.add("vector", lambda e, j=j, par=par: e.tensor_copy(
                                sS[par][:, j * 512:(j + 1) * 512], ps_s[par][0:128, 0:512]),
                                deps=[mm] + [d for d in [sS_rd.get(par)] if d])
                        evs.append(ev)
                    rm = s.add("vector", lambda e, kwin=kwin, par=par: e.reduce_max(
                        rowm[par][:, 0:1], sS[par][:, 0:kwin], axis=mb.AxisListType.X),
                        deps=evs)
                    nb = s.add("vector", lambda e, par=par: e.tensor_scalar(
                        negb[par][:, 0:1], rowm[par][:, 0:1], -INVSQ, None, OP.mult),
                        deps=[rm])
                    ex = s.add("scalar", lambda e, kwin=kwin, par=par: e.activation(
                        pB[par][:, 0:kwin], sS[par][:, 0:kwin], AF.Exp, scale=CON[INVSQ],
                        bias=negb[par][:, 0:1], accum_out=rs[par][:, 0:1]),
                        deps=[nb] + [d for d in [pB_rd.get(par), rc_pr.get(par)] if d])
                    sS_rd[par] = ex
                    rc = s.add("vector", lambda e, par=par: e.reciprocal(
                        rinv[par][:, 0:1], rs[par][:, 0:1]), deps=[ex])
                    rc_pr[par] = rc
                    pn = s.add("vector", lambda e, kwin=kwin, par=par: e.tensor_scalar(
                        pB[par][:, 0:kwin], pB[par][:, 0:kwin], rinv[par][:, 0:1], None,
                        OP.mult), deps=[rc])
                    tp_evacs = []
                    for jb in range(nkb * 4):
                        tp = s.add("tensor", lambda e, jb=jb, par=par: e.transpose(
                            psT[par][:, :], pB[par][:, jb * 128:(jb + 1) * 128], idb[:, :]),
                            deps=[pn] + (tp_evacs[-1:] if tp_evacs else
                                         [d for d in [tev_pr.get(par)] if d]))
                        tev = s.add("vector", lambda e, jb=jb, par=par: e.tensor_copy(
                            pT[par][:, jb * 128:(jb + 1) * 128], psT[par][:, :]),
                            deps=[tp] + [d for d in [pT_rd.get(par)] if d])
                        tp_evacs.append(tev)
                    pB_rd[par] = tp_evacs[-1]
                    tev_pr[par] = tp_evacs[-1]
                    for vj in range(2):
                        mmo = None
                        for jb in range(nkb * 4):
                            g = kbs[jb // 4] * 4 + jb % 4
                            mmo = s.add("tensor", lambda e, jb=jb, g=g, ci=ci, vj=vj,
                                        par=par: e.matmul(
                                psO[par][0:128, (ci * 2 + vj) * 128:(ci * 2 + vj + 1) * 128],
                                vT[vj][:, g * 128:(g + 1) * 128],
                                pT[par][:, jb * 128:(jb + 1) * 128],
                                start=(jb == 0), stop=(jb == nkb * 4 - 1)),
                                deps=([tp_evacs[-1]] + [d for d in [psO_rd.get(par)] if d])
                                     if jb == 0 else [])
                        last_pv = mmo
                    pT_rd[par] = last_pv
                cp = []
                for j2 in range(2):
                    cp.append(s.add("vector", lambda e, j2=j2, par=par: e.tensor_copy(
                        a1sb[par][:, j2 * 128:(j2 + 1) * 128],
                        psO[par][0:128, j2 * 128:(j2 + 1) * 128]),
                        deps=[last_pv] + [d for d in [dst_pr.get(par)] if d]))
                dst = []
                for j2 in range(2):
                    dst.append(s.add("vector", lambda e, j2=j2, par=par: e.scalar_tensor_tensor(
                        dsb[par][:, j2 * 128:(j2 + 1) * 128],
                        psO[par][0:128, (2 + j2) * 128:(3 + j2) * 128], nlamC[:, 0:1],
                        a1sb[par][:, j2 * 128:(j2 + 1) * 128], OP.mult, OP.add),
                        deps=[cp[j2], t8] + [d for d in [oo_pr.get(par)] if d]))
                dst_pr[par] = dst[-1]
                m_sq = None
                for j2 in range(2):
                    sqs = s.add("scalar", lambda e, j2=j2, par=par: e.activation(
                        sqb[par][:, :], dsb[par][:, j2 * 128:(j2 + 1) * 128], AF.Square),
                        deps=[dst[j2], m_sq] + ([rw1_pr] if (j2 == 0 and rw1_pr) else []))
                    m_sq = s.add("tensor", lambda e, j2=j2, par=par: e.matmul(
                        psr[0:1, 0:128], onescb[:, 0:1], sqb[par][:, :],
                        start=(j2 == 0), stop=(j2 == 1)), deps=[sqs])
                rw1 = s.add("scalar", lambda e: e.activation(
                    r2[0:1, :], psr[0:1, :], AF.Ln, scale=CON[1.0 / 256][0:1, :],
                    bias=CON[EPS][0:1, :]), deps=[m_sq])
                rw1_pr = rw1
                rw2 = s.add("scalar", lambda e, par=par: e.activation(
                    rr0[par][0:1, :], r2[0:1, :], AF.Exp, scale=CON[-0.5][0:1, :],
                    bias=CON[float(np.log(1.0 - LAMBDA_INIT))][0:1, :]), deps=[rw1])
                bmm = s.add("tensor", lambda e, par=par: e.matmul(
                    psO[par][0:128, 0:128], onesr[0:1, :], rr0[par][0:1, :],
                    start=True, stop=True), deps=[rw2, dst[0], dst[1], cp[0], cp[1]])
                bev = s.add("vector", lambda e, par=par: e.tensor_copy(
                    rbF[par][:, :], psO[par][0:128, 0:128]), deps=[bmm])
                psO_rd[par] = bev
                oos = []
                for j2 in range(2):
                    oo = s.add("vector", lambda e, j2=j2, par=par: e.scalar_tensor_tensor(
                        oF[par][:, j2 * 128:(j2 + 1) * 128],
                        dsb[par][:, j2 * 128:(j2 + 1) * 128], subw[:, j2:j2 + 1],
                        rbF[par][:, :], OP.mult, OP.mult),
                        deps=[bev] + [d for d in [oo_pr.get(par)] if d])
                    dd = s.dma("gpsimd", AT_out[p * 256 + j2 * 128:p * 256 + (j2 + 1) * 128,
                                                qt * 128:(qt + 1) * 128],
                               oF[par][:, j2 * 128:(j2 + 1) * 128], deps=[oo])
                    oos.append(dd)
                oo_pr[par] = oos[-1]
        s.emit(ctx)
    return nc
def build_l8():
    nc = bass.Bass("TRN2", target_bir_lowering=False, debug=False)
    H_in = nc.dram_tensor("h", [D, TC], F32R, kind="ExternalInput").ap()
    AT_in = nc.dram_tensor("at", [D, TC], F32R, kind="ExternalInput").ap()
    X2T = nc.dram_tensor("x2", [D, TC], F32R, kind="ExternalInput").ap()
    OPW = nc.dram_tensor("opw", [128, 16 * 16 * 128], F32R, kind="ExternalInput").ap()
    OPB = nc.dram_tensor("opb", [128, 16], F32, kind="ExternalInput").ap()
    MLNW = nc.dram_tensor("mlnw", [128, 16], F32, kind="ExternalInput").ap()
    FC1 = nc.dram_tensor("fc1", [128, 128 * 16 * 128], F32R, kind="ExternalInput").ap()
    FC2 = nc.dram_tensor("fc2", [128, 16 * 64 * 128], F32R, kind="ExternalInput").ap()
    RW = nc.dram_tensor("rw", [128, 256], F32R, kind="ExternalInput").ap()
    EBR = nc.dram_tensor("ebr", [1, 16], F32R, kind="ExternalInput").ap()
    EGU = nc.dram_tensor("egu", [128, 17 * 12 * 16 * 128], F32R, kind="ExternalInput").ap()
    EDN = nc.dram_tensor("edn", [128, 17 * 16 * 6 * 128], F32R, kind="ExternalInput").ap()
    IDF = nc.dram_tensor("idf", [128, 128], F32, kind="ExternalInput").ap()
    ONESC = nc.dram_tensor("onesc", [128, 1], F32R, kind="ExternalInput").ap()
    ONESR = nc.dram_tensor("onesr", [1, 128], F32R, kind="ExternalInput").ap()
    X4_out = nc.dram_tensor("x4", [D, TC], F32, kind="ExternalOutput").ap()
    CMBD = nc.dram_tensor("cmbd", [16, TC], F32, kind="Internal").ap()
    if DEBUG_L8:
        DB_h2 = nc.dram_tensor("dbg_h2", [D, TC], F32, kind="ExternalOutput").ap()
        DB_x3 = nc.dram_tensor("dbg_x3", [D, TC], F32, kind="ExternalOutput").ap()
        DB_cmb = nc.dram_tensor("dbg_cmb", [16, TC], F32, kind="ExternalOutput").ap()
        DB_hf = nc.dram_tensor("dbg_hf", [D, TC], F32, kind="ExternalOutput").ap()

    s = Sched(nc)
    ctx = ExitStack()
    with ctx:
        RA = ctx.enter_context(nc.sbuf_tensor("RA", [128, 8192], F32R)).ap()
        RH = ctx.enter_context(nc.sbuf_tensor("RH", [128, 16384], F32R)).ap()
        RX = ctx.enter_context(nc.sbuf_tensor("RX", [128, 16384], F32R)).ap()
        SN = ctx.enter_context(nc.sbuf_tensor("SN", [128, 1536], F32R)).ap()
        MISCF = ctx.enter_context(nc.sbuf_tensor("MISCF", [128, 1024], F32)).ap()
        ROLLF = ctx.enter_context(nc.sbuf_tensor("ROLLF", [128, 2048], F32)).ap()
        WB = [ctx.enter_context(nc.sbuf_tensor(f"WB{j}", [128, 2048], F32R)).ap() for j in range(2)]
        WD = [ctx.enter_context(nc.sbuf_tensor(f"WD{j}", [128, 512], F32R)).ap() for j in range(2)]
        rwsb = ctx.enter_context(nc.sbuf_tensor("rwsb", [128, 256], F32R)).ap()
        ebrsb = ctx.enter_context(nc.sbuf_tensor("ebrsb", [1, 16], F32R)).ap()
        CMT = ctx.enter_context(nc.sbuf_tensor("CMT", [128, 128], F32)).ap()
        CBB = [ctx.enter_context(nc.sbuf_tensor("CBB0", [128, 1024], F32)).ap()]
        opb = ctx.enter_context(nc.sbuf_tensor("opbsb", [128, 16], F32)).ap()
        mlnw = ctx.enter_context(nc.sbuf_tensor("mlnwsb", [128, 16], F32)).ap()
        idf = ctx.enter_context(nc.sbuf_tensor("idfsb", [128, 128], F32)).ap()
        onesc = ctx.enter_context(nc.sbuf_tensor("onescsb", [128, 1], F32R)).ap()
        onesr = ctx.enter_context(nc.sbuf_tensor("onesrsb", [1, 128], F32R)).ap()
        ps = [ctx.enter_context(nc.psum_tensor(f"ps{j}", [128, 512], F32)).ap() for j in range(4)]
        psr0 = ctx.enter_context(nc.psum_tensor("psr0", [1, 512], F32)).ap()
        psr1 = ctx.enter_context(nc.psum_tensor("psr1", [1, 512], F32)).ap()
        psq = ctx.enter_context(nc.psum_tensor("psq", [128, 16], F32)).ap()
        pst = ctx.enter_context(nc.psum_tensor("pst", [128, 128], F32)).ap()

        sq = SN[:, 0:512]
        rbt = SN[:, 512:1024]
        mbt = SN[:, 1024:1536]
        rr0 = SN[0:1, 0:512]
        rr1 = SN[0:1, 1024:1536]
        r2 = MISCF[0:1, 0:512]
        r3 = MISCF[0:1, 512:1024]
        scoT = MISCF[:, 0:128]
        keys = MISCF[:, 128:144]
        cmb = MISCF[:, 144:160]
        mks = MISCF[:, 160:224]
        wks = MISCF[:, 224:228]
        wsum = MISCF[:, 228:229]
        winv = MISCF[:, 229:230]
        ebb = MISCF[:, 230:246]
        t16 = MISCF[:, 246:262]
        rmx = MISCF[:, 262:263]
        wn = MISCF[:, 263:264]
        rx2 = [ROLLF[:, 0:512], ROLLF[:, 512:1024]]
        rot = [ROLLF[:, 1024:1536], ROLLF[:, 1536:2048]]

        def tile(buf, k, w=1024):
            return buf[:, k * w:(k + 1) * w]

        for (d_, src_) in [(opb, OPB), (mlnw, MLNW), (idf, IDF), (onesc, ONESC),
                           (onesr, ONESR)]:
            s.dma("sync", d_[:, :], src_[:, :])
        s.dma("sync", rwsb[:, :], RW[:, :])
        CON = make_consts(s, nc, ctx, [EPS, 1.0 / D, -0.5, 1.0 / 2048.0])
        load_fm(s, RX, AT_in, 16, TC, eng="gpsimd")
        s.barrier()

        def stats_half(buf, a, with_mean):
            m_sum = None
            m_sq = None
            prev_mm = None
            for i in range(16):
                if with_mean:
                    m_sum = s.add("tensor", lambda e, i=i, a=a, buf=buf: e.matmul(
                        psr0[0:1, 0:512], onesc[:, 0:1], tile(buf, i)[:, a:a + 512],
                        start=(i == 0), stop=(i == 15)))
                sqs = s.add("scalar", lambda e, i=i, a=a, buf=buf: e.activation(
                    sq[:, 0:512], tile(buf, i).bitcast(F32)[:, a:a + 512], AF.Square),
                    deps=[prev_mm])
                m_sq = s.add("tensor", lambda e, i=i: e.matmul(
                    psr1[0:1, 0:512], onesc[:, 0:1], sq[:, 0:512],
                    start=(i == 0), stop=(i == 15)), deps=[sqs])
                prev_mm = m_sq
            return m_sum, m_sq

        def rows_rms(stop_sq):
            t1 = s.add("scalar", lambda e: e.activation(
                r2[0:1, 0:512], psr1[0:1, 0:512], AF.Ln,
                scale=CON[1.0 / D][0:1, :], bias=CON[EPS][0:1, :]), deps=[stop_sq])
            return s.add("scalar", lambda e: e.activation(
                rr0[0:1, 0:512], r2[0:1, 0:512], AF.Exp, scale=CON[-0.5][0:1, :]),
                deps=[t1])

        def rows_ln(stop_sum, stop_sq):
            inv_n = 1.0 / D
            m2 = s.add("scalar", lambda e: e.activation(
                r2[0:1, 0:512], psr0[0:1, 0:512], AF.Square, scale=CON[1.0 / D][0:1, :]),
                deps=[stop_sum])
            var = s.add("vector", lambda e: e.scalar_tensor_tensor(
                r3[0:1, 0:512], psr1[0:1, 0:512], float(inv_n), r2[0:1, 0:512],
                OP.mult, OP.subtract), deps=[m2, stop_sq])
            t1 = s.add("scalar", lambda e: e.activation(
                r2[0:1, 0:512], r3[0:1, 0:512], AF.Ln, bias=CON[EPS][0:1, :]), deps=[var])
            t2 = s.add("scalar", lambda e: e.activation(
                rr0[0:1, 0:512], r2[0:1, 0:512], AF.Exp, scale=CON[-0.5][0:1, :]), deps=[t1])
            t3 = s.add("vector", lambda e: e.scalar_tensor_tensor(
                rr1[0:1, 0:512], psr0[0:1, 0:512], float(-inv_n),
                rr0[0:1, 0:512].bitcast(F32), OP.mult, OP.mult), deps=[t2])
            return t3

        prev_mm = {}
        prev_ev = {}
        huse = {}
        for ob in range(16):
            buf = ob % 2
            lw = s.dma("sync", WB[buf][:, :], OPW[:, ob * 2048:(ob + 1) * 2048],
                       deps=[d for d in [prev_mm.get(buf)] if d])
            for ci in range(2):
                ta = ci * 512
                pps = ps[ci]
                mm = None
                for kt in range(16):
                    mm = s.add("tensor", lambda e, kt=kt, ta=ta, pps=pps, buf=buf: e.matmul(
                        pps[0:128, 0:512], WB[buf][:, kt * 128:(kt + 1) * 128],
                        RX[:, kt * 1024 + ta:kt * 1024 + ta + 512],
                        start=(kt == 0), stop=(kt == 15)),
                        deps=([lw, prev_ev.get(ci)] if kt == 0 else []))
                lh = s.dma("gpsimd", rx2[ci][:, :], H_in[ob * 128:(ob + 1) * 128, ta:ta + 512],
                           deps=[d for d in [huse.get(ci)] if d])
                e1 = s.add("vector", lambda e, ob=ob, ta=ta, pps=pps, ci=ci: e.scalar_tensor_tensor(
                    tile(RH, ob)[:, ta:ta + 512], pps[0:128, 0:512], opb[:, ob:ob + 1],
                    rx2[ci][:, :], OP.add, OP.add), deps=[mm, lh])
                huse[ci] = e1
                prev_ev[ci] = e1
                prev_mm[buf] = mm
        s.barrier()

        if DEBUG_L8:
            for i in range(16):
                s.dma("gpsimd", DB_h2[i * 128:(i + 1) * 128, :], tile(RH, i).bitcast(F32))
            s.barrier()
        for tch in range(2):
            a = tch * 512
            _, ssq = stats_half(RH, a, with_mean=False)
            rw_ = rows_rms(ssq)
            bcast_row(s, nc, rr0, onesr, ps[3], rbt, 512, deps=[rw_])
            s.barrier()
            for i in range(16):
                s.add("vector", lambda e, i=i, a=a: e.scalar_tensor_tensor(
                    tile(RX, i)[:, a:a + 512], tile(RH, i).bitcast(F32)[:, a:a + 512],
                    mlnw[:, i:i + 1], rbt.bitcast(F32), OP.mult, OP.mult))
            s.barrier()

        xuse = {}
        for i in range(16):
            for ci in range(2):
                ta = ci * 512
                lx = s.dma("gpsimd", rx2[ci][:, :], X2T[i * 128:(i + 1) * 128, ta:ta + 512],
                           deps=[d for d in [xuse.get(ci)] if d])
                xuse[ci] = s.add("vector", lambda e, i=i, ta=ta, ci=ci: e.tensor_tensor(
                    tile(RH, i)[:, ta:ta + 512], tile(RH, i).bitcast(F32)[:, ta:ta + 512],
                    rx2[ci][:, :], OP.add), deps=[lx])
        s.barrier()

        prev_mm = {}
        prev_ev = {}
        prev_mm_d = {}
        prev_ev_d = {}
        gate_rd = {}
        m_rd = {}
        m_wr = {}
        for grp in range(16):
            for part in range(2):
                for gg4 in range(4):
                    gg = (64 * part) + grp * 4 + gg4
                    buf = gg4 % 2
                    lwg = s.dma("sync", WB[buf][:, :], FC1[:, gg * 2048:(gg + 1) * 2048],
                                deps=[d for d in [prev_mm.get(buf)] if d])
                    for ci in range(2):
                        ta = ci * 512
                        pps = ps[ci]
                        mm = None
                        for kt in range(16):
                            mm = s.add("tensor", lambda e, kt=kt, ta=ta, pps=pps, buf=buf: e.matmul(
                                pps[0:128, 0:512], WB[buf][:, kt * 128:(kt + 1) * 128],
                                RX[:, kt * 1024 + ta:kt * 1024 + ta + 512],
                                start=(kt == 0), stop=(kt == 15)),
                                deps=([lwg, prev_ev.get(ci)] if kt == 0 else []))
                        if part == 0:
                            ev = s.add("scalar", lambda e, gg4=gg4, ta=ta, pps=pps: e.activation(
                                tile(RA, gg4)[:, ta:ta + 512], pps[0:128, 0:512], AF.Silu),
                                deps=[mm] + [d for d in [gate_rd.get((gg4, ci))] if d])
                        else:
                            ev = s.add("vector", lambda e, gg4=gg4, ta=ta, pps=pps: e.tensor_tensor(
                                tile(RA, 4 + gg4)[:, ta:ta + 512], pps[0:128, 0:512],
                                tile(RA, gg4).bitcast(F32)[:, ta:ta + 512], OP.mult),
                                deps=[mm] + [d for d in [m_rd.get((gg4, ci))] if d])
                            gate_rd[(gg4, ci)] = ev
                            m_wr[(gg4, ci)] = ev
                        prev_ev[ci] = ev
                        prev_mm[buf] = mm
            for od in range(16):
                buf = od % 2
                lw2 = s.dma("sync", WD[buf][:, :],
                            FC2[:, od * 8192 + grp * 512:od * 8192 + (grp + 1) * 512],
                            deps=[d for d in [prev_mm_d.get(buf)] if d])
                for ci in range(2):
                    ta = ci * 512
                    pps = ps[2 + ci]
                    mm = None
                    for kt in range(4):
                        mm = s.add("tensor", lambda e, kt=kt, ta=ta, pps=pps, buf=buf: e.matmul(
                            pps[0:128, 0:512], WD[buf][:, kt * 128:(kt + 1) * 128],
                            RA[:, (4 + kt) * 1024 + ta:(4 + kt) * 1024 + ta + 512],
                            start=(kt == 0), stop=(kt == 3)),
                            deps=([lw2, prev_ev_d.get(ci)] if kt == 0 else []) +
                                 [d for d in [m_wr.get((kt, ci))] if d])
                        m_rd[(kt, ci)] = mm
                    ev = s.add("vector", lambda e, od=od, ta=ta, pps=pps: e.tensor_tensor(
                        tile(RH, od)[:, ta:ta + 512], pps[0:128, 0:512],
                        tile(RH, od).bitcast(F32)[:, ta:ta + 512], OP.add), deps=[mm])
                    prev_ev_d[ci] = ev
                    prev_mm_d[buf] = mm
        s.barrier()

        for tch in range(2):
            a = tch * 512
            ssum, ssq = stats_half(RH, a, with_mean=True)
            rw_ = rows_ln(ssum, ssq)
            bcast_row(s, nc, rr0, onesr, ps[3], rbt, 512, deps=[rw_])
            bcast_row(s, nc, rr1, onesr, ps[2], mbt, 512, deps=[rw_])
            s.barrier()
            for i in range(16):
                t1 = s.add("vector", lambda e, i=i, a=a: e.tensor_tensor(
                    sq[:, 0:512], tile(RH, i).bitcast(F32)[:, a:a + 512],
                    rbt.bitcast(F32), OP.mult))
                s.add("vector", lambda e, i=i, a=a: e.tensor_tensor(
                    tile(RX, i)[:, a:a + 512], sq[:, 0:512].bitcast(F32),
                    mbt.bitcast(F32), OP.add), deps=[t1])
            s.barrier()

        if DEBUG_L8:
            for i in range(16):
                s.dma("gpsimd", DB_hf[i * 128:(i + 1) * 128, :], tile(RH, i).bitcast(F32))
            for i in range(16):
                s.dma("gpsimd", DB_x3[i * 128:(i + 1) * 128, :], tile(RX, i).bitcast(F32))
            s.barrier()
        leb = s.dma("sync", ebrsb[0:1, :], EBR[0:1, :])
        ebm = s.add("tensor", lambda e: e.matmul(
            psq[0:128, 0:16], onesr[0:1, :], ebrsb[0:1, 0:16],
            start=True, stop=True), deps=[leb])
        s.add("vector", lambda e: e.tensor_copy(ebb[:, :], psq[0:128, 0:16]), deps=[ebm])
        s.barrier()
        for tt in range(8):
            mm = None
            for kt in range(16):
                mm = s.add("tensor", lambda e, kt=kt, tt=tt: e.matmul(
                    psq[0:128, 0:16], RX[:, kt * 1024 + tt * 128:kt * 1024 + (tt + 1) * 128],
                    rwsb[:, kt * 16:(kt + 1) * 16], start=(kt == 0), stop=(kt == 15)))
            sg_ = s.add("scalar", lambda e, tt=tt: e.activation(
                scoT[:, tt * 16:(tt + 1) * 16], psq[0:128, 0:16], AF.Sigmoid), deps=[mm])
            kk = s.add("vector", lambda e, tt=tt: e.tensor_tensor(
                keys[:, :], scoT[:, tt * 16:(tt + 1) * 16], ebb[:, :], OP.add), deps=[sg_])
            prevk = kk
            for k4 in range(4):
                rm_ = s.add("vector", lambda e: e.reduce_max(
                    rmx[:, 0:1], keys[:, :], axis=mb.AxisListType.X), deps=[prevk])
                mk = s.add("vector", lambda e, k4=k4: e.tensor_scalar(
                    mks[:, k4 * 16:(k4 + 1) * 16], keys[:, :], rmx[:, 0:1], None,
                    OP.is_equal), deps=[rm_])
                wkt = s.add("vector", lambda e, k4=k4, tt=tt: e.tensor_tensor(
                    t16[:, :], mks[:, k4 * 16:(k4 + 1) * 16],
                    scoT[:, tt * 16:(tt + 1) * 16], OP.mult), deps=[mk])
                ws_ = s.add("vector", lambda e, k4=k4: e.tensor_reduce(
                    wks[:, k4:k4 + 1], t16[:, :], mb.AxisListType.X, OP.add), deps=[wkt])
                prevk = s.add("vector", lambda e, k4=k4: e.scalar_tensor_tensor(
                    keys[:, :], mks[:, k4 * 16:(k4 + 1) * 16], float(NEG), keys[:, :],
                    OP.mult, OP.add), deps=[ws_])
            ws2 = s.add("vector", lambda e: e.tensor_reduce(
                wsum[:, 0:1], wks[:, 0:4], mb.AxisListType.X, OP.add), deps=[prevk])
            ws3 = s.add("vector", lambda e: e.tensor_scalar(
                wsum[:, 0:1], wsum[:, 0:1], 1e-20, None, OP.add), deps=[ws2])
            wi_ = s.add("vector", lambda e: e.reciprocal(winv[:, 0:1], wsum[:, 0:1]),
                        deps=[ws3])
            prevc = None
            for k4 in range(4):
                wnk = s.add("vector", lambda e, k4=k4: e.tensor_tensor(
                    wn[:, 0:1], wks[:, k4:k4 + 1], winv[:, 0:1], OP.mult),
                    deps=[wi_] + [d for d in [prevc] if d])
                if k4 == 0:
                    prevc = s.add("vector", lambda e, k4=k4: e.tensor_scalar(
                        cmb[:, :], mks[:, k4 * 16:(k4 + 1) * 16], wn[:, 0:1], None,
                        OP.mult), deps=[wnk])
                else:
                    prevc = s.add("vector", lambda e, k4=k4: e.scalar_tensor_tensor(
                        cmb[:, :], mks[:, k4 * 16:(k4 + 1) * 16], wn[:, 0:1],
                        cmb[:, :], OP.mult, OP.add), deps=[wnk])
            tp = s.add("tensor", lambda e: e.transpose(
                pst[0:16, 0:128], cmb[:, :], idf[:, :]), deps=[prevc])
            tev = s.add("vector", lambda e, tt=tt: e.tensor_copy(
                CMT[0:16, 0:128], pst[0:16, 0:128]), deps=[tp])
            s.dma("gpsimd", CMBD[0:16, tt * 128:(tt + 1) * 128],
                  CMT[0:16, 0:128], deps=[tev])
            s.barrier()

        if DEBUG_L8:
            s.dma("gpsimd", DB_cmb[0:16, :], CMBD[0:16, :])
            s.barrier()
        for i in range(16):
            s.add("vector", lambda e, i=i: e.tensor_copy(tile(RH, i), tile(RX, i)))
        s.barrier()
        cuse = None
        prev_ev_dn = {}
        prev_mm = {}
        prev_ev = {}
        prev_mm_d = {}
        gate_rd = {}
        m_rd = {}
        m_wr = {}
        for ex in range(17):
            shared = (ex == 16)
            if not shared:
                lcb = s.dma("gpsimd", CBB[0][:, :],
                            CMBD[ex:ex + 1, :].broadcast_to([128, TC]),
                            deps=[d for d in [cuse] if d])
            for half in range(2):
                for obl in range(6):
                    part = obl // 3
                    ob = part * 6 + half * 3 + (obl % 3)
                    buf = obl % 2
                    lw = s.dma("sync", WB[buf][:, :],
                               EGU[:, (ex * 12 + ob) * 2048:(ex * 12 + ob + 1) * 2048],
                               deps=[d for d in [prev_mm.get(buf)] if d])
                    for ci in range(2):
                        ta = ci * 512
                        pps = ps[ci]
                        mm = None
                        for kt in range(16):
                            mm = s.add("tensor", lambda e, kt=kt, ta=ta, pps=pps, buf=buf: e.matmul(
                                pps[0:128, 0:512], WB[buf][:, kt * 128:(kt + 1) * 128],
                                RX[:, kt * 1024 + ta:kt * 1024 + ta + 512],
                                start=(kt == 0), stop=(kt == 15)),
                                deps=([lw, prev_ev.get(ci)] if kt == 0 else []))
                        if part == 0:
                            ev = s.add("scalar", lambda e, obl=obl, ta=ta, pps=pps: e.activation(
                                tile(RA, obl)[:, ta:ta + 512], pps[0:128, 0:512], AF.Silu),
                                deps=[mm] + [d for d in [gate_rd.get((obl, ci))] if d])
                        else:
                            sl = obl - 3
                            ev = s.add("vector", lambda e, sl=sl, ta=ta, pps=pps: e.tensor_tensor(
                                tile(RA, 4 + sl)[:, ta:ta + 512], pps[0:128, 0:512],
                                tile(RA, sl).bitcast(F32)[:, ta:ta + 512], OP.mult),
                                deps=[mm] + [d for d in [m_rd.get((sl, ci))] if d])
                            gate_rd[(sl, ci)] = ev
                            if not shared:
                                ev = s.add("vector", lambda e, sl=sl, ta=ta: e.tensor_tensor(
                                    tile(RA, 4 + sl)[:, ta:ta + 512],
                                    tile(RA, 4 + sl).bitcast(F32)[:, ta:ta + 512],
                                    CBB[0][:, ta:ta + 512], OP.mult), deps=[ev, lcb])
                                cuse = ev
                            m_wr[(sl, ci)] = ev
                        prev_ev[ci] = ev
                        prev_mm[buf] = mm
                for od in range(16):
                    buf = od % 2
                    lw = s.dma("sync", WD[buf][:, 0:384],
                               EDN[:, (ex * 16 + od) * 768 + half * 384:
                                   (ex * 16 + od) * 768 + (half + 1) * 384],
                               deps=[d for d in [prev_mm_d.get(buf)] if d])
                    for ci in range(2):
                        ta = ci * 512
                        pps = ps[2 + ci]
                        mm = None
                        for kt in range(3):
                            mm = s.add("tensor", lambda e, kt=kt, ta=ta, pps=pps, buf=buf: e.matmul(
                                pps[0:128, 0:512], WD[buf][:, kt * 128:(kt + 1) * 128],
                                RA[:, (4 + kt) * 1024 + ta:(4 + kt) * 1024 + ta + 512],
                                start=(kt == 0), stop=(kt == 2)),
                                deps=([lw, prev_ev_dn.get((od, ci))] if kt == 0 else []) +
                                     [d for d in [m_wr.get((kt, ci))] if d])
                            m_rd[(kt, ci)] = mm
                        ev = s.add("vector", lambda e, od=od, ta=ta, pps=pps: e.tensor_tensor(
                            tile(RH, od)[:, ta:ta + 512], pps[0:128, 0:512],
                            tile(RH, od).bitcast(F32)[:, ta:ta + 512], OP.add), deps=[mm])
                        prev_ev_dn[(od, ci)] = ev
                        prev_mm_d[buf] = mm
        s.barrier()

        for tch in range(2):
            a = tch * 512
            ssum, ssq = stats_half(RH, a, with_mean=True)
            rw_ = rows_ln(ssum, ssq)
            bcast_row(s, nc, rr0, onesr, ps[3], rbt, 512, deps=[rw_])
            bcast_row(s, nc, rr1, onesr, ps[2], mbt, 512, deps=[rw_])
            s.barrier()
            prev_dma = None
            for i in range(16):
                t1 = s.add("vector", lambda e, i=i, a=a: e.tensor_tensor(
                    sq[:, 0:512], tile(RH, i).bitcast(F32)[:, a:a + 512],
                    rbt.bitcast(F32), OP.mult))
                t2 = s.add("vector", lambda e, i=i, a=a: e.tensor_tensor(
                    rot[i % 2][:, :], sq[:, 0:512].bitcast(F32),
                    mbt.bitcast(F32), OP.add),
                    deps=[t1] + [d for d in [prev_dma] if d])
                prev_dma = s.dma("gpsimd", X4_out[i * 128:(i + 1) * 128, a:a + 512],
                                 rot[i % 2][:, :], deps=[t2])
            s.barrier()
        s.emit(ctx)
    return nc
